# version 11
# baseline (speedup 1.0000x reference)
import contextlib
import numpy as np
import ml_dtypes
import concourse.bass as bass
import concourse.mybir as mybir
from concourse.bass_utils import run_bass_kernel_spmd

F32 = mybir.dt.float32
BF16 = mybir.dt.bfloat16
AF = mybir.ActivationFunctionType
ALU = mybir.AluOpType
AXX = mybir.AxisListType.X
BF = ml_dtypes.bfloat16

NCORES = 8
T = 2048
D = 2048
NH = 16
HD = 128
NTC = T // 128
NDC = D // 128
I0 = 6464
I0P = 6528
NC0 = 51
I1 = 5461
I1P = 5504
NC1 = 43
CH = 128
SCALE0 = float(HD) ** -0.5

_N_STAGES = 99
_DBG = False

ENGS = ("sync", "scalar", "vector", "tensor", "gpsimd")


class Res:
    __slots__ = ("ap", "wdep", "rdeps")

    def __init__(self, ap=None):
        self.ap = ap
        self.wdep = None
        self.rdeps = []


class KB:
    def __init__(self, nc):
        self.nc = nc
        self.ops = {e: [] for e in ENGS}
        self.cnt = {}
        self.S = {}
        self.lastw = {}

    def op(self, eng, fn):
        self.ops[eng].append(fn)

    def inc(self, sem, kk=1):
        self.cnt[sem] = self.cnt.get(sem, 0) + kk
        return self.cnt[sem]

    def val(self, sem):
        return self.cnt.get(sem, 0)

    def wait_raw(self, eng, sem, v):
        if not v or v <= 0:
            return
        key = (eng, sem)
        if self.lastw.get(key, 0) >= v:
            return
        self.lastw[key] = v
        self.op(eng, lambda E, s=sem, vv=v: E.wait_ge(self.S[s], vv))

    OWN = {"tensor": "mm"}

    def waits(self, eng, deps):
        own = self.OWN.get(eng)
        best = {}
        for d in deps:
            if d is None:
                continue
            s, v = d
            if s == own:
                continue
            if v and (s not in best or best[s] < v):
                best[s] = v
        for s, v in best.items():
            self.wait_raw(eng, s, v)

    def emit(self, eng, meth, *a, sem=None, inc=1, **kw):
        tgt = self.inc(sem, inc) if sem else None

        def fn(E, meth=meth, a=a, kw=kw, sem=sem, inc=inc):
            ins = getattr(E, meth)(*a, **kw)
            if sem:
                ins.then_inc(self.S[sem], inc)

        self.op(eng, fn)
        return (sem, tgt) if sem else None

    def do(self, eng, meth, *a, reads=(), writes=(), sem=None, inc=1,
           extra=(), **kw):
        deps = []
        for r in reads:
            deps.append(r.wdep)
        for w in writes:
            deps.append(w.wdep)
            deps.extend(w.rdeps)
        deps.extend(extra)
        self.waits(eng, deps)
        v = self.emit(eng, meth, *a, sem=sem, inc=inc, **kw)
        if v is not None:
            for r in reads:
                r.rdeps.append(v)
            for w in writes:
                w.wdep = v
                w.rdeps = []
        return v

    def dma(self, eng, out, in_, sem, reads=(), writes=(), extra=()):
        return self.do(eng, "dma_start", reads=reads, writes=writes, sem=sem,
                       inc=16, extra=extra, out=out, in_=in_)

    def dma_batch(self, eng, sem, items, writes=(), reads=(), extra=()):
        deps = list(extra)
        for r in reads:
            deps.append(r.wdep)
        for w in writes:
            deps.append(w.wdep)
            deps.extend(w.rdeps)
        self.waits(eng, deps)
        v = None
        for out, in_ in items:
            v = self.emit(eng, "dma_start", sem=sem, inc=16, out=out, in_=in_)
        for r in reads:
            r.rdeps.append(v)
        for w in writes:
            w.wdep = v
            w.rdeps = []
        return v

    def fence(self):
        vals = {s: self.cnt.get(s, 0) for s in self.S}
        for eng in ENGS:
            own = self.OWN.get(eng)
            for s, v in vals.items():
                if s != own:
                    self.wait_raw(eng, s, v)


def _build_program():
    nc = bass.Bass("TRN2", target_bir_lowering=False)
    k = KB(nc)

    x_in = nc.dram_tensor("x_in", [T, D], F32, kind="ExternalInput").ap()
    xT_in = nc.dram_tensor("xT_in", [D, T], F32, kind="ExternalInput").ap()
    y_out = nc.dram_tensor("y_out", [T, D], F32, kind="ExternalOutput").ap()
    dbg = Res(nc.dram_tensor("dbg", [128, 8192], F32, kind="ExternalOutput").ap())

    W = {}
    for nm, shp in [
        ("l0_wq", [D, D]), ("l0_wk", [D, D]), ("l0_wv", [D, D]), ("l0_wo", [D, D]),
        ("l0_wg", [I0, D]), ("l0_wu", [I0, D]), ("l0_wd", [D, I0]),
        ("l1_wq", [D, D]), ("l1_wk", [D, D]), ("l1_wv", [D, D]), ("l1_wo", [D, D]),
        ("l1_wg1", [D, HD]), ("l1_wg2", [HD, D]),
        ("l1_w1", [I1, D]), ("l1_w3", [I1, D]), ("l1_w2", [D, I1]),
    ]:
        W[nm] = nc.dram_tensor(nm, shp, F32, kind="ExternalInput").ap()

    C = {}
    for nm, shp, dt in [
        ("ident_b", [128, 128], BF16), ("ident_f", [128, 128], F32),
        ("ones_f", [128, 512], F32),
        ("bq_sb", [128, NH], F32), ("bk_sb", [128, NH], F32),
        ("bv_sb", [128, NH], F32),
        ("bo_rep", [128, D], F32), ("nw_rep", [128, D], F32),
        ("cs_t", [128, T], F32), ("sc_t", [128, T], F32),
        ("mrot", [128, 128], F32),
        ("dmask", [128, NH * 128], F32),
        ("deck", [128, NH], F32), ("decq", [128, NH], F32),
        ("flag", [128, 1], F32), ("flagi", [128, 1], F32),
    ]:
        C[nm] = nc.dram_tensor(nm, shp, dt, kind="ExternalInput").ap()

    spill_x = nc.dram_tensor("spill_x", [T, D], F32).ap()
    xt_il = nc.dram_tensor("xt_il", [2 * D, T], BF16).ap()
    wgT_q = nc.dram_tensor("wgT_q", [D, I0P], BF16).ap()
    wuT_q = nc.dram_tensor("wuT_q", [D, I0P], BF16).ap()
    wdT_q = nc.dram_tensor("wdT_q", [I0P, D], BF16).ap()
    w1T_q = nc.dram_tensor("w1T_q", [D, I1P], BF16).ap()
    w3T_q = nc.dram_tensor("w3T_q", [D, I1P], BF16).ap()
    w2T_q = nc.dram_tensor("w2T_q", [I1P, D], BF16).ap()
    xt_h = nc.dram_tensor("xt_h", [D, T], BF16).ap()
    xt_l = nc.dram_tensor("xt_l", [D, T], BF16).ap()
    cc0_in = [nc.dram_tensor(f"cc0i{g}", [128, 1], F32).ap() for g in range(NH)]
    cc0_out = [nc.dram_tensor(f"cc0o{g}", [128, 1], F32).ap() for g in range(NH)]
    cc1_in = [nc.dram_tensor(f"cc1i{g}", [128, 128], F32).ap()
              for g in range(NH)]
    cc1_out = [nc.dram_tensor(f"cc1o{g}", [128, 128], F32).ap()
               for g in range(NH)]
    PAIRS = [[0, 1], [2, 3], [4, 5], [6, 7]]

    with contextlib.ExitStack() as ctx:
        for s in ["sx", "sw", "sh", "sh2", "so", "sc", "st", "sq",
                  "mm", "vv", "aa", "gg", "cc"]:
            k.S[s] = ctx.enter_context(nc.semaphore(s))

        def sbuf(name, shape, dt=F32):
            return Res(ctx.enter_context(nc.sbuf_tensor(name, shape, dt)))

        x_tm = [sbuf(f"x_tm{i}", [128, D]) for i in range(NTC)]
        consts = {}
        for nm, shape, dt in [("ident_b", [128, 128], BF16),
                              ("ident_f", [128, 128], F32),
                              ("ones_f", [128, 512], F32),
                              ("bq_sb", [128, NH], F32),
                              ("bk_sb", [128, NH], F32),
                              ("bv_sb", [128, NH], F32),
                              ("flag", [128, 1], F32),
                              ("flagi", [128, 1], F32)]:
            consts[nm] = sbuf("c_" + nm, shape, dt)
        pb = [Res(ctx.enter_context(nc.psum_tensor(f"pb{i}", [128, 512], F32)))
              for i in range(8)]

        k.dma_batch("sync", "sx",
                    [(x_tm[i].ap[:, :], x_in[i * 128:(i + 1) * 128, :])
                     for i in range(NTC)]
                    + [(consts[nm].ap[:, :], C[nm]) for nm in consts],
                    writes=x_tm + list(consts.values()))

        with nc.sbuf_tensor("sp_f", [128, T], F32) as sp_f_t, \
             nc.sbuf_tensor("sp_h", [128, T], BF16) as sp_h_t, \
             nc.sbuf_tensor("sp_l", [128, T], BF16) as sp_l_t:
            sp_f, sp_h, sp_l = Res(sp_f_t), Res(sp_h_t), Res(sp_l_t)
            for dc in range(NDC):
                k.dma("sync", sp_f.ap[:, :], xT_in[dc * 128:(dc + 1) * 128, :],
                      "sx", writes=[sp_f])
                k.do("scalar", "activation", out=sp_h.ap[:, :],
                     in_=sp_f.ap[:, :], func=AF.Identity, reads=[sp_f],
                     writes=[sp_h], sem="aa")
                k.do("vector", "tensor_sub", sp_l.ap[:, :], sp_f.ap[:, :],
                     sp_h.ap[:, :], reads=[sp_f, sp_h], writes=[sp_l], sem="vv")
                k.dma_batch("gpsimd", "so",
                            [(xt_il[dc * 256:dc * 256 + 128, :],
                              sp_h.ap[:, :]),
                             (xt_il[dc * 256 + 128:dc * 256 + 256, :],
                              sp_l.ap[:, :])],
                            reads=[sp_h, sp_l])
        v_xt_spill = ("so", k.val("so"))
        k.fence()

        cfg = dict(nc=nc, k=k, W=W, C=C, x_tm=x_tm, pb=pb, ctx=ctx,
                   consts=consts, xt_h=xt_h, xt_l=xt_l, spill_x=spill_x,
                   cc0_in=cc0_in, cc0_out=cc0_out, cc1_in=cc1_in,
                   cc1_out=cc1_out, PAIRS=PAIRS, v_xt_spill=v_xt_spill,
                   xt_il=xt_il,
                   wgT_q=wgT_q, wuT_q=wuT_q, wdT_q=wdT_q, x_in=x_in,
                   w1T_q=w1T_q, w3T_q=w3T_q, w2T_q=w2T_q, dbg=dbg)

        if _N_STAGES >= 1:
            _stage_l0_attn(cfg)
        k.fence()
        if _N_STAGES >= 2:
            _stage_l0_ffn(cfg)
            k.fence()
        if _N_STAGES >= 3:
            _stage_l1_attn(cfg)
            k.fence()
        if _N_STAGES >= 4:
            _stage_l1_ffn(cfg)
            k.fence()

        vst = k.dma_batch("gpsimd", "st",
                          [(y_out[i * 128:(i + 1) * 128, :], x_tm[i].ap[:, :])
                           for i in range(NTC)], reads=x_tm,
                          extra=[v_xt_spill])
        k.waits("gpsimd", [vst])

        with nc.Block() as block:
            for en in ENGS:
                def mk(en):
                    def f(E):
                        for fn in k.ops[en]:
                            fn(E)
                    return f
                getattr(block, en)(mk(en))

    return nc


def _split(k, src, hi, lo, scale=None, cols=None, src_cols=None):
    sl = slice(None) if cols is None else cols
    ssl = sl if src_cols is None else src_cols
    if scale is None:
        k.do("scalar", "activation", out=hi.ap[:, sl], in_=src.ap[:, ssl],
             func=AF.Identity, reads=[src], writes=[hi], sem="aa")
        k.do("vector", "tensor_sub", lo.ap[:, sl], src.ap[:, ssl],
             hi.ap[:, sl], reads=[src, hi], writes=[lo], sem="vv")
    else:
        k.do("scalar", "activation", out=hi.ap[:, sl], in_=src.ap[:, ssl],
             func=AF.Identity, scale=float(scale), reads=[src], writes=[hi],
             sem="aa")
        k.do("vector", "scalar_tensor_tensor", lo.ap[:, sl], src.ap[:, ssl],
             float(scale), hi.ap[:, sl], ALU.mult, ALU.subtract,
             reads=[src, hi], writes=[lo], sem="vv")


def _transpose16(k, pb, ident_b, src, dst, bankno=4, n=16, src_off=0,
                 dst_off=0):
    for grp in range((n + 3) // 4):
        bank = pb[bankno + grp % 2]
        lo = grp * 4
        hi = min(lo + 4, n)
        for q in range(hi - lo):
            c0 = src_off + (lo + q) * 128
            k.do("tensor", "matmul", bank.ap[:, q * 128:(q + 1) * 128],
                 src.ap[:, c0:c0 + 128], ident_b.ap[:, :],
                 start=True, stop=True, reads=[src, ident_b], writes=[bank],
                 sem="mm" if q == hi - lo - 1 else None)
        w = (hi - lo) * 128
        d0 = dst_off + lo * 128
        k.do("scalar", "activation", out=dst.ap[:, d0:d0 + w],
             in_=bank.ap[:, 0:w], func=AF.Identity, reads=[bank],
             writes=[dst], sem="aa")


def _stage_l0_attn(cfg):
    k = cfg["k"]; nc = cfg["nc"]; W = cfg["W"]; C = cfg["C"]
    x_tm = cfg["x_tm"]; pb = cfg["pb"]; co = cfg["consts"]
    xt_h = cfg["xt_h"]; xt_l = cfg["xt_l"]
    ident_b = co["ident_b"]; ones_f = co["ones_f"]; flag = co["flag"]
    flagi = co["flagi"]
    xt_il = cfg["xt_il"]
    bq_sb = co["bq_sb"]; bk_sb = co["bk_sb"]; bv_sb = co["bv_sb"]
    cc0_in = cfg["cc0_in"]; cc0_out = cfg["cc0_out"]; PAIRS = cfg["PAIRS"]

    with contextlib.ExitStack() as st:
        def sbuf(name, shape, dt=F32):
            return Res(st.enter_context(nc.sbuf_tensor(name, shape, dt)))

        w_f = sbuf("a0_wf", [128, D])
        w_h = sbuf("a0_wh", [128, 1024], BF16)
        w_l = sbuf("a0_wl", [128, 1024], BF16)
        wT = [[sbuf(f"a0_wT{i}{hl}", [128, D], BF16) for hl in range(2)]
              for i in range(2)]
        xtb2 = [sbuf(f"a0_xtb{i}", [128, 8 * 512], BF16) for i in range(2)]
        t1 = sbuf("a0_t1", [128, 512])
        t2 = sbuf("a0_t2", [128, 512])
        t3 = sbuf("a0_t3", [128, 8])
        kv = sbuf("a0_kv", [128, T])
        kvcar = sbuf("a0_kvcar", [128, 1])
        zsum = sbuf("a0_zsum", [128, 1])
        car0 = sbuf("a0_car0", [128, 1])
        ctx_h = [sbuf(f"a0_cxh{i}", [128, T], BF16) for i in range(2)]
        ctx_l = [sbuf(f"a0_cxl{i}", [128, T], BF16) for i in range(2)]
        woT_h = sbuf("a0_woTh", [128, 2 * 512], BF16)
        woT_l = sbuf("a0_woTl", [128, 2 * 512], BF16)
        sq = sbuf("a0_sq", [128, 1])
        msc = sbuf("a0_msc", [128, 2 * NTC])
        k.fence()

        def load_wT(wap, j, dst):
            k.dma("sync", w_f.ap[:, :], wap[j * 128:(j + 1) * 128, :], "sw",
                  writes=[w_f])
            for hf in range(2):
                sc = slice(hf * 1024, (hf + 1) * 1024)
                _split(k, w_f, w_h, w_l, cols=slice(0, 1024), src_cols=sc)
                _transpose16(k, pb, ident_b, w_h, dst[0], n=8,
                             dst_off=hf * 1024)
                _transpose16(k, pb, ident_b, w_l, dst[1], n=8,
                             dst_off=hf * 1024)

        def load_xtb(tau, quarter, par):
            xtb = xtb2[par]
            c0 = tau * 512
            r0 = quarter * 1024
            k.dma_batch("sync" if par == 0 else "scalar",
                        "sh" if par == 0 else "sh2",
                        [(xtb.ap[:, :].rearrange("p (b c) -> p b c", c=512),
                          xt_il[r0:r0 + 1024, c0:c0 + 512].rearrange(
                              "(b p) c -> p b c", p=128))],
                        writes=[xtb], extra=[cfg["v_xt_spill"]])

        def proj_half(wpair, quarter, bankno, first, par):
            xtb = xtb2[par]
            bank = pb[bankno]
            wh, wl = wpair
            for i in range(4):
                dc = quarter * 4 + i
                lh = wh.ap[:, dc * 128:(dc + 1) * 128]
                ll = wl.ap[:, dc * 128:(dc + 1) * 128]
                rh = xtb.ap[:, (2 * i) * 512:(2 * i + 1) * 512]
                rl = xtb.ap[:, (2 * i + 1) * 512:(2 * i + 2) * 512]
                st0 = first and i == 0
                last = (quarter == 3 and i == 3)
                k.do("tensor", "matmul", bank.ap[:, :], lh, rh,
                     start=st0, stop=False, reads=[wh, xtb], writes=[bank])
                k.do("tensor", "matmul", bank.ap[:, :], lh, rl,
                     start=False, stop=False, reads=[wh, xtb], writes=[bank])
                k.do("tensor", "matmul", bank.ap[:, :], ll, rh,
                     start=False, stop=last,
                     reads=[wh, wl, xtb], writes=[bank],
                     sem="mm" if i == 3 else None)

        def phi_chain(bank, bias, j, out):
            k.do("vector", "tensor_scalar", out.ap[:, :], bank.ap[:, :],
                 bias.ap[:, j:j + 1], 0.0, ALU.add, ALU.min,
                 reads=[bank, bias], writes=[out], sem="vv")
            k.do("vector", "tensor_scalar", t2.ap[:, :], bank.ap[:, :],
                 bias.ap[:, j:j + 1], 0.0, ALU.add, ALU.max,
                 reads=[bank, bias], writes=[t2], sem="vv")
            k.do("scalar", "activation", out=out.ap[:, :], in_=out.ap[:, :],
                 func=AF.Exp, reads=[out], writes=[out], sem="aa")
            k.do("vector", "tensor_add", out.ap[:, :], out.ap[:, :],
                 t2.ap[:, :], reads=[t2], writes=[out], sem="vv")

        k.dma("sync", w_f.ap[:, :], C["bo_rep"], "sw", writes=[w_f])
        for tc in range(NTC):
            k.do("vector", "tensor_add", x_tm[tc].ap[:, :], x_tm[tc].ap[:, :],
                 w_f.ap[:, :], reads=[w_f], writes=[x_tm[tc]], sem="vv")

        for j in range(NH):
            load_wT(W["l0_wk"], j, wT[0])
            load_wT(W["l0_wv"], j, wT[1])
            k.do("vector", "memset", kvcar.ap[:, :], 0.0, writes=[kvcar],
                 sem="vv")
            k.do("vector", "memset", zsum.ap[:, :], 0.0, writes=[zsum],
                 sem="vv")
            for tau in range(4):
                bk0, bk1 = (tau % 2) * 2, (tau % 2) * 2 + 1
                for qtr in range(4):
                    par = (tau * 4 + qtr) % 2
                    load_xtb(tau, qtr, par)
                    proj_half(wT[0], qtr, bk0, first=(qtr == 0), par=par)
                    proj_half(wT[1], qtr, bk1, first=(qtr == 0), par=par)
                cols = slice(tau * 512, (tau + 1) * 512)
                phi_chain(pb[bk0], bk_sb, j, t1)
                k.do("vector", "tensor_scalar", t2.ap[:, :], pb[bk1].ap[:, :],
                     bv_sb.ap[:, j:j + 1], None, ALU.add,
                     reads=[pb[bk1], bv_sb], writes=[t2], sem="vv")
                k.do("vector", "tensor_mul", t2.ap[:, :], t1.ap[:, :],
                     t2.ap[:, :], reads=[t1], writes=[t2], sem="vv")
                k.do("vector", "reduce_sum", t3.ap[:, 0:1], t2.ap[:, :],
                     axis=AXX, reads=[t2], writes=[t3], sem="vv")
                k.do("vector", "tensor_add", zsum.ap[:, :], zsum.ap[:, :],
                     t3.ap[:, 0:1], reads=[t3], writes=[zsum], sem="vv")
                k.do("vector", "tensor_tensor_scan", kv.ap[:, cols],
                     ones_f.ap[:, :], t2.ap[:, :], kvcar.ap[:, :],
                     ALU.mult, ALU.add, reads=[ones_f, t2, kvcar],
                     writes=[kv], sem="vv")
                k.do("vector", "tensor_copy", kvcar.ap[:, :],
                     kv.ap[:, tau * 512 + 511:tau * 512 + 512],
                     reads=[kv], writes=[kvcar], sem="vv")

            k.do("gpsimd", "tensor_scalar", t3.ap[:, 1:2], zsum.ap[:, :],
                 flagi.ap[:, 0:1], None, ALU.mult, reads=[zsum, flagi],
                 writes=[t3], sem="gg")
            vz = k.dma("gpsimd", cc0_in[j][:, :], t3.ap[:, 1:2], "sc",
                       reads=[t3])
            k.do("gpsimd", "collective_compute", "AllReduce", ALU.add,
                 PAIRS, [cc0_in[j][:, :]], [cc0_out[j][:, :]], sem="cc",
                 extra=[vz])
            k.dma("gpsimd", car0.ap[:, :], cc0_out[j][:, :], "sc",
                  writes=[car0], extra=[("cc", k.val("cc"))])
            k.do("gpsimd", "tensor_scalar", car0.ap[:, :], car0.ap[:, :],
                 flag.ap[:, 0:1], None, ALU.mult, reads=[flag], writes=[car0],
                 sem="gg")

            load_wT(W["l0_wq"], j, wT[0])
            slot = j % 2
            for tau in range(4):
                bk0 = (tau % 2) * 2
                for qtr in range(4):
                    par = (tau * 4 + qtr) % 2
                    load_xtb(tau, qtr, par)
                    proj_half(wT[0], qtr, bk0, first=(qtr == 0), par=par)
                cols = slice(tau * 512, (tau + 1) * 512)
                phi_chain(pb[bk0], bq_sb, j, t1)
                k.do("vector", "tensor_scalar", t2.ap[:, :], kv.ap[:, cols],
                     car0.ap[:, 0:1], None, ALU.add, reads=[kv, car0],
                     writes=[t2], sem="vv")
                k.do("vector", "tensor_mul", t2.ap[:, :], t1.ap[:, :],
                     t2.ap[:, :], reads=[t1], writes=[t2], sem="vv")
                _split(k, t2, ctx_h[slot], ctx_l[slot], scale=SCALE0,
                       cols=cols, src_cols=slice(0, 512))

            if j % 2 == 1:
                for et in range(4):
                    for s2 in range(2):
                        jo = j - 1 + s2
                        k.dma_batch(
                            "sync", "sw",
                            [(w_f.ap[:, q * 128:(q + 1) * 128],
                              W["l0_wo"][et * 512 + q * 128:
                                         et * 512 + (q + 1) * 128,
                                         jo * 128:(jo + 1) * 128])
                             for q in range(4)], writes=[w_f])
                        _split(k, w_f, w_h, w_l, cols=slice(0, 512))
                        _transpose16(k, pb, ident_b, w_h, woT_h, n=4,
                                     dst_off=s2 * 512)
                        _transpose16(k, pb, ident_b, w_l, woT_l, n=4,
                                     dst_off=s2 * 512)
                    for tc in range(NTC):
                        bank = pb[6 + tc % 2]
                        first = True
                        for s2 in range(2):
                            ch = ctx_h[s2]; cl = ctx_l[s2]
                            csl = slice(tc * 128, (tc + 1) * 128)
                            wsl = slice(s2 * 512, (s2 + 1) * 512)
                            k.do("tensor", "matmul", bank.ap[:, :],
                                 ch.ap[:, csl], woT_h.ap[:, wsl],
                                 start=first, stop=False,
                                 reads=[ch, woT_h], writes=[bank])
                            first = False
                            k.do("tensor", "matmul", bank.ap[:, :],
                                 ch.ap[:, csl], woT_l.ap[:, wsl],
                                 start=False, stop=False,
                                 reads=[ch, woT_l], writes=[bank])
                            k.do("tensor", "matmul", bank.ap[:, :],
                                 cl.ap[:, csl], woT_h.ap[:, wsl],
                                 start=False, stop=(s2 == 1),
                                 reads=[ch, cl, woT_h, woT_l], writes=[bank],
                                 sem="mm" if s2 == 1 else None)
                        ecols = slice(et * 512, (et + 1) * 512)
                        k.do("vector", "tensor_add", x_tm[tc].ap[:, ecols],
                             x_tm[tc].ap[:, ecols], bank.ap[:, :],
                             reads=[bank], writes=[x_tm[tc]], sem="vv")

        _layernorm(k, x_tm, msc, w_f, sq)


def _layernorm(k, x_tm, msc, lnscr, sq, chunks=None):
    n = len(x_tm) if chunks is None else len(chunks)
    tcs = range(len(x_tm)) if chunks is None else chunks
    for ii, tc in enumerate(tcs):
        k.do("vector", "reduce_sum", msc.ap[:, ii:ii + 1], x_tm[tc].ap[:, :],
             axis=AXX, reads=[x_tm[tc]], writes=[msc], sem="vv")
        k.do("scalar", "activation", out=lnscr.ap[:, :],
             in_=x_tm[tc].ap[:, :], func=AF.Square,
             accum_out=sq.ap[:, 0:1], reads=[x_tm[tc]], writes=[lnscr, sq],
             sem="aa")
        k.do("vector", "tensor_copy", msc.ap[:, n + ii:n + ii + 1],
             sq.ap[:, 0:1], reads=[sq], writes=[msc], sem="vv")
    k.do("vector", "tensor_scalar", msc.ap[:, 0:n], msc.ap[:, 0:n],
         1.0 / D, None, ALU.mult, reads=[msc], writes=[msc], sem="vv")
    k.do("vector", "tensor_scalar", msc.ap[:, n:2 * n], msc.ap[:, n:2 * n],
         1.0 / D, None, ALU.mult, reads=[msc], writes=[msc], sem="vv")
    k.do("vector", "tensor_mul", lnscr.ap[:, 0:n], msc.ap[:, 0:n],
         msc.ap[:, 0:n], reads=[msc], writes=[lnscr], sem="vv")
    k.do("vector", "tensor_sub", msc.ap[:, n:2 * n], msc.ap[:, n:2 * n],
         lnscr.ap[:, 0:n], reads=[lnscr], writes=[msc], sem="vv")
    k.do("vector", "tensor_scalar", msc.ap[:, n:2 * n], msc.ap[:, n:2 * n],
         1e-5, None, ALU.add, reads=[msc], writes=[msc], sem="vv")
    k.do("scalar", "activation", out=msc.ap[:, n:2 * n],
         in_=msc.ap[:, n:2 * n], func=AF.Sqrt, reads=[msc],
         writes=[msc], sem="aa")
    k.do("vector", "reciprocal", msc.ap[:, n:2 * n], msc.ap[:, n:2 * n],
         reads=[msc], writes=[msc], sem="vv")
    for ii, tc in enumerate(tcs):
        k.do("vector", "tensor_scalar", x_tm[tc].ap[:, :], x_tm[tc].ap[:, :],
             msc.ap[:, ii:ii + 1], msc.ap[:, n + ii:n + ii + 1],
             ALU.subtract, ALU.mult, reads=[msc], writes=[x_tm[tc]], sem="vv")


def _stage_l0_ffn(cfg):
    k = cfg["k"]; nc = cfg["nc"]; W = cfg["W"]
    x_tm = cfg["x_tm"]; pb = cfg["pb"]; co = cfg["consts"]
    ident_b = co["ident_b"]; ones_f = co["ones_f"]
    spill_x = cfg["spill_x"]
    wgT_q = cfg["wgT_q"]; wuT_q = cfg["wuT_q"]; wdT_q = cfg["wdT_q"]
    dbg = cfg["dbg"]
    EPS = 1e-5
    HT = [(i * 512, 512) for i in range(12)] + [(6144, 384)]

    with contextlib.ExitStack() as st:
        def sbuf(name, shape, dt=F32):
            return Res(st.enter_context(nc.sbuf_tensor(name, shape, dt)))

        w_f = sbuf("f0_wf", [128, D])
        wq_b = sbuf("f0_wqb", [128, D], BF16)
        tx_b = sbuf("f0_txb", [128, D], BF16)
        t_a = sbuf("f0_ta", [128, D])
        t_b = sbuf("f0_tb", [128, D])
        i_t = sbuf("f0_it", [128, D], mybir.dt.int32)
        scq = sbuf("f0_scq", [128, 8])
        rx = sbuf("f0_rx", [128, 12])
        mx2 = sbuf("f0_mx2", [128, 2])
        acc = sbuf("f0_acc", [128, 2])
        sm = sbuf("f0_sm", [1, 8])
        msc = sbuf("f0_msc", [128, 8])
        sq = sbuf("f0_sq", [128, 1])
        dbg_sb = sbuf("f0_dbgsb", [128, 512])
        k.fence()

        def dbg_cp(c0, src_ap, src_res, w, psum=False):
            if not _DBG:
                return
            if psum:
                k.do("vector", "tensor_copy", dbg_sb.ap[:, 0:w], src_ap,
                     reads=[src_res], writes=[dbg_sb], sem="vv")
                k.dma("gpsimd", dbg.ap[:, c0:c0 + w], dbg_sb.ap[:, 0:w],
                      "st", reads=[dbg_sb])
            else:
                k.do("vector", "tensor_copy", dbg_sb.ap[:, 0:w], src_ap,
                     reads=[src_res], writes=[dbg_sb], sem="vv")
                k.dma("gpsimd", dbg.ap[:, c0:c0 + w], dbg_sb.ap[:, 0:w],
                      "st", reads=[dbg_sb])

        k.dma_batch("gpsimd", "so",
                    [(spill_x[i * 128:(i + 1) * 128, :], x_tm[i].ap[:, :])
                     for i in range(NTC)], reads=x_tm)
        v_spill = ("so", k.val("so"))

        def wchunks(nm):
            if nm in ("l0_wg", "l0_wu"):
                for r0 in range(0, I0, 128):
                    nr = min(128, I0 - r0)
                    yield W[nm][r0:r0 + nr, :], nr, D
            else:
                for r0 in range(0, D, 128):
                    for c0 in range(0, I0, 2048):
                        w = min(2048, I0 - c0)
                        yield W[nm][r0:r0 + 128, c0:c0 + w], 128, w

        for mi, nm in enumerate(["l0_wg", "l0_wu", "l0_wd"]):
            k.do("vector", "memset", acc.ap[:, 0:1], 0.0, writes=[acc],
                 sem="vv")
            for sl, nr, wcols in wchunks(nm):
                k.dma("sync", w_f.ap[0:nr, 0:wcols], sl, "sw", writes=[w_f])
                k.do("scalar", "activation", out=t_a.ap[0:nr, 0:wcols],
                     in_=w_f.ap[0:nr, 0:wcols], func=AF.Abs, reads=[w_f],
                     writes=[t_a], sem="aa")
                k.do("vector", "tensor_reduce", acc.ap[0:nr, 1:2],
                     t_a.ap[0:nr, 0:wcols], axis=AXX, op=ALU.add,
                     reads=[t_a], writes=[acc], sem="vv")
                k.do("vector", "tensor_add", acc.ap[0:nr, 0:1],
                     acc.ap[0:nr, 0:1], acc.ap[0:nr, 1:2], reads=[acc],
                     writes=[acc], sem="vv")
            k.do("tensor", "matmul", pb[5].ap[0:1, 0:1], acc.ap[:, 0:1],
                 ones_f.ap[:, 0:1], start=True, stop=True,
                 reads=[acc, ones_f], writes=[pb[5]], sem="mm")
            k.do("vector", "tensor_scalar", sm.ap[0:1, 0:1],
                 pb[5].ap[0:1, 0:1], 1.0 / (I0 * D), EPS, ALU.mult, ALU.max,
                 reads=[pb[5]], writes=[sm], sem="vv")
            k.do("vector", "reciprocal", sm.ap[0:1, 1:2], sm.ap[0:1, 0:1],
                 reads=[sm], writes=[sm], sem="vv")
            k.do("tensor", "matmul", pb[5].ap[:, 0:2], ones_f.ap[0:1, 0:128],
                 sm.ap[0:1, 0:2], start=True, stop=True, reads=[ones_f, sm],
                 writes=[pb[5]], sem="mm")
            k.do("vector", "tensor_copy", scq.ap[:, 2 * mi:2 * mi + 1],
                 pb[5].ap[:, 1:2], reads=[pb[5]], writes=[scq], sem="vv")
            k.do("vector", "tensor_copy", scq.ap[:, 2 * mi + 1:2 * mi + 2],
                 pb[5].ap[:, 0:1], reads=[pb[5]], writes=[scq], sem="vv")

        def quant_chain(eng, src, dst_ap, nr, wcols, sc_col, lo, hi,
                        dst_res, src_c0=0):
            sem = {"vector": "vv", "gpsimd": "gg"}[eng]
            k.do(eng, "tensor_scalar", t_a.ap[0:nr, 0:wcols],
                 src.ap[0:nr, src_c0:src_c0 + wcols], sc_col[0:nr, :], 0.5,
                 ALU.mult, ALU.add, reads=[src, scq, rx], writes=[t_a],
                 sem=sem)
            k.do(eng, "tensor_scalar_add", i_t.ap[0:nr, 0:wcols],
                 t_a.ap[0:nr, 0:wcols], 0.0, reads=[t_a], writes=[i_t],
                 sem=sem)
            k.do(eng, "tensor_tensor", t_b.ap[0:nr, 0:wcols],
                 i_t.ap[0:nr, 0:wcols], t_a.ap[0:nr, 0:wcols], ALU.is_gt,
                 reads=[i_t, t_a], writes=[t_b], sem=sem)
            k.do(eng, "tensor_sub", t_a.ap[0:nr, 0:wcols],
                 i_t.ap[0:nr, 0:wcols], t_b.ap[0:nr, 0:wcols],
                 reads=[i_t, t_b], writes=[t_a], sem=sem)
            k.do(eng, "tensor_scalar", dst_ap, t_a.ap[0:nr, 0:wcols],
                 float(hi), float(lo), ALU.min, ALU.max, reads=[t_a],
                 writes=[dst_res], sem=sem)

        def transpose_store(ntile, dram, r0c0):
            _transpose16(k, pb, ident_b, wq_b, tx_b, n=ntile)
            h0, c0 = r0c0
            out = dram[h0:h0 + ntile * 128, c0:c0 + 128].rearrange(
                "(dc p) c -> p dc c", p=128)
            in_ = tx_b.ap[:, 0:ntile * 128].rearrange(
                "p (dc c) -> p dc c", c=128)
            k.dma("gpsimd", out, in_, "sq", reads=[tx_b])

        qeng = ["vector", "vector"]
        qi = 0
        for nm, dram, sc_i in [("l0_wg", wgT_q, 0), ("l0_wu", wuT_q, 2)]:
            for ci, r0 in enumerate(range(0, I0P, 128)):
                nr = min(128, I0 - r0) if r0 < I0 else 0
                if nr < 128:
                    k.do("vector", "memset", wq_b.ap[nr:128, :], 0.0,
                         writes=[wq_b], sem="vv")
                if nr > 0:
                    k.dma("sync", w_f.ap[0:nr, :], W[nm][r0:r0 + nr, :], "sw",
                          writes=[w_f])
                    quant_chain(qeng[qi % 2], w_f, wq_b.ap[0:nr, :], nr, D,
                                scq.ap[:, sc_i:sc_i + 1], -1.0, 1.0, wq_b)
                    qi += 1
                _transpose16(k, pb, ident_b, wq_b, tx_b, n=16)
                out = dram[0:D, ci * 128:(ci + 1) * 128].rearrange(
                    "(dc p) c -> p dc c", p=128)
                in_ = tx_b.ap[:, :].rearrange("p (dc c) -> p dc c", c=128)
                k.dma("gpsimd", out, in_, "sq", reads=[tx_b])
        for ci, r0 in enumerate(range(0, D, 128)):
            for pi, c0 in enumerate(range(0, I0P, 2048)):
                wreal = max(0, min(2048, I0 - c0))
                wpad = min(2048, I0P - c0)
                ntile = wpad // 128
                if wreal < wpad:
                    k.do("vector", "memset", wq_b.ap[:, wreal:wpad], 0.0,
                         writes=[wq_b], sem="vv")
                k.dma("sync", w_f.ap[:, 0:wreal], W["l0_wd"][r0:r0 + 128,
                                                             c0:c0 + wreal],
                      "sw", writes=[w_f])
                quant_chain(qeng[qi % 2], w_f, wq_b.ap[:, 0:wreal], 128,
                            wreal, scq.ap[:, 4:5], -1.0, 1.0, wq_b)
                qi += 1
                transpose_store(ntile, wdT_q, (c0, ci * 128))

        dbg_cp(0, scq.ap[:, 0:8], scq, 8)
        v_qcache = ("sq", k.val("sq"))

        for g in range(8):
            G = [2 * g, 2 * g + 1]
            scr = [i for i in range(NTC) if i not in G]
            gu_t = [scr[0:4], scr[4:8]]
            wg_t = [G[0], scr[8]]
            wu_t = [G[1], scr[9]]
            nT_t = scr[10]
            nbf_t = scr[11]
            n2bf_t = [scr[12], scr[13]]
            xg_t = [scr[12], scr[13]]
            xr_t = [scr[10], scr[11]]
            n2T_t = gu_t[0]
            dn_t = gu_t[1] + [wg_t[0], wg_t[1], wu_t[0]]

            def bfv(ti):
                return x_tm[ti].ap.bitcast(BF16)

            k.dma_batch("sync", "sx",
                        [(x_tm[xg_t[tc]].ap[:, :],
                          spill_x[G[tc] * 128:(G[tc] + 1) * 128, :])
                         for tc in range(2)],
                        writes=[x_tm[xg_t[0]], x_tm[xg_t[1]]],
                        extra=[v_spill])
            for tc in range(2):
                xres = x_tm[xg_t[tc]]
                k.do("vector", "reduce_max", rx.ap[:, 8 + tc:9 + tc],
                     xres.ap[:, :], axis=AXX, apply_absolute_value=True,
                     reads=[xres], writes=[rx], sem="vv")
                k.do("vector", "tensor_scalar", rx.ap[:, 8 + tc:9 + tc],
                     rx.ap[:, 8 + tc:9 + tc], EPS, 1.0 / 127.0, ALU.max,
                     ALU.mult, reads=[rx], writes=[rx], sem="vv")
                k.do("vector", "tensor_mul", rx.ap[:, 0 + tc:1 + tc],
                     rx.ap[:, 8 + tc:9 + tc], scq.ap[:, 1:2], reads=[rx, scq],
                     writes=[rx], sem="vv")
                k.do("vector", "tensor_mul", rx.ap[:, 2 + tc:3 + tc],
                     rx.ap[:, 8 + tc:9 + tc], scq.ap[:, 3:4], reads=[rx, scq],
                     writes=[rx], sem="vv")
                k.do("vector", "reciprocal", rx.ap[:, 10 + tc:11 + tc],
                     rx.ap[:, 8 + tc:9 + tc], reads=[rx], writes=[rx],
                     sem="vv")
                n_ap = bfv(nbf_t)[:, tc * D:(tc + 1) * D]
                quant_chain("vector", xres, n_ap, 128, D,
                            rx.ap[:, 10 + tc:11 + tc], -128.0, 127.0,
                            x_tm[nbf_t])
                if g == 0 and tc == 0:
                    dbg_cp(8, rx.ap[:, 0:12], rx, 12)
                    dbg_cp(512, bfv(nbf_t)[:, 0:512], x_tm[nbf_t], 512)
                for dcg in range(4):
                    bank = pb[4]
                    for q in range(4):
                        dc = dcg * 4 + q
                        k.do("tensor", "matmul",
                             bank.ap[:, q * 128:(q + 1) * 128],
                             bfv(nbf_t)[:, tc * D + dc * 128:
                                        tc * D + (dc + 1) * 128],
                             ident_b.ap[:, :], start=True, stop=True,
                             reads=[x_tm[nbf_t], ident_b], writes=[bank],
                             sem="mm" if q == 3 else None)
                    for q in range(4):
                        dc = dcg * 4 + q
                        k.do("scalar", "activation",
                             out=bfv(nT_t)[:, dc * 256 + tc * 128:
                                           dc * 256 + (tc + 1) * 128],
                             in_=bank.ap[:, q * 128:(q + 1) * 128],
                             func=AF.Identity, reads=[bank],
                             writes=[x_tm[nT_t]], sem="aa")

            k.do("vector", "memset", mx2.ap[:, :], 0.0, writes=[mx2],
                 sem="vv")
            for ht, (h0, wdt) in enumerate(HT):
                items = []
                for dram, tiles in ((wgT_q, wg_t), (wuT_q, wu_t)):
                    for half in range(2):
                        outv = bfv(tiles[half])[:, 0:8 * 512].rearrange(
                            "p (dc c) -> p dc c", c=512)[:, :, 0:wdt]
                        inv = dram[half * 1024:(half + 1) * 1024,
                                   h0:h0 + wdt].rearrange(
                            "(dc p) c -> p dc c", p=128)
                        items.append((outv, inv))
                k.dma_batch("sync", "sh", items,
                            writes=[x_tm[wg_t[0]], x_tm[wg_t[1]],
                                    x_tm[wu_t[0]], x_tm[wu_t[1]]],
                            extra=[v_qcache])
                for tc in range(2):
                    for pi, (dram, tiles) in enumerate(
                            ((wgT_q, wg_t), (wuT_q, wu_t))):
                        bank = pb[pi]
                        grp_reads = [x_tm[nT_t], x_tm[tiles[0]],
                                     x_tm[tiles[1]]]
                        for dc in range(16):
                            til = tiles[dc // 8]
                            lhs = bfv(nT_t)[:, dc * 256 + tc * 128:
                                            dc * 256 + (tc + 1) * 128]
                            rhs = bfv(til)[:, (dc % 8) * 512:
                                           (dc % 8) * 512 + wdt]
                            k.do("tensor", "matmul", bank.ap[:, 0:wdt], lhs,
                                 rhs, start=(dc == 0), stop=(dc == 15),
                                 reads=(grp_reads if dc == 15
                                        else [x_tm[nT_t], x_tm[til]]),
                                 writes=[bank],
                                 sem="mm" if dc == 15 else None)
                    k.do("scalar", "activation", out=t_a.ap[:, 0:wdt],
                         in_=pb[0].ap[:, 0:wdt], func=AF.Silu,
                         scale=rx.ap[:, 0 + tc:1 + tc], reads=[pb[0], rx],
                         writes=[t_a], sem="aa")
                    k.do("vector", "tensor_scalar", t_b.ap[:, 0:wdt],
                         pb[1].ap[:, 0:wdt], rx.ap[:, 2 + tc:3 + tc], None,
                         ALU.mult, reads=[pb[1], rx], writes=[t_b], sem="vv")
                    if g == 0 and ht == 0 and tc == 0:
                        dbg_cp(1024, t_a.ap[:, 0:512], t_a, 512)
                        dbg_cp(1536, t_b.ap[:, 0:512], t_b, 512)
                    gut = x_tm[gu_t[tc][ht // 4]]
                    gcol = (ht % 4) * 512
                    k.do("vector", "tensor_mul",
                         gut.ap[:, gcol:gcol + wdt], t_a.ap[:, 0:wdt],
                         t_b.ap[:, 0:wdt], reads=[t_a, t_b], writes=[gut],
                         sem="vv")
                    if g == 0 and ht == 0 and tc == 0:
                        dbg_cp(2048, gut.ap[:, 0:512], gut, 512)
                    k.do("vector", "reduce_max", rx.ap[:, 4 + tc:5 + tc],
                         gut.ap[:, gcol:gcol + wdt], axis=AXX,
                         apply_absolute_value=True, reads=[gut], writes=[rx],
                         sem="vv")
                    k.do("vector", "tensor_max", mx2.ap[:, tc:tc + 1],
                         mx2.ap[:, tc:tc + 1], rx.ap[:, 4 + tc:5 + tc],
                         reads=[rx], writes=[mx2], sem="vv")

            for tc in range(2):
                k.do("vector", "tensor_scalar", rx.ap[:, 8 + tc:9 + tc],
                     mx2.ap[:, tc:tc + 1], EPS, 1.0 / 127.0, ALU.max,
                     ALU.mult, reads=[mx2], writes=[rx], sem="vv")
                k.do("vector", "tensor_mul", rx.ap[:, 6 + tc:7 + tc],
                     rx.ap[:, 8 + tc:9 + tc], scq.ap[:, 5:6], reads=[rx, scq],
                     writes=[rx], sem="vv")
                k.do("vector", "reciprocal", rx.ap[:, 10 + tc:11 + tc],
                     rx.ap[:, 8 + tc:9 + tc], reads=[rx], writes=[rx],
                     sem="vv")
                for ht, (h0, wdt) in enumerate(HT):
                    gut = x_tm[gu_t[tc][ht // 4]]
                    gcol = (ht % 4) * 512
                    nti = n2bf_t[h0 // 4096]
                    ncol = h0 % 4096
                    quant_chain("vector", gut,
                                bfv(nti)[:, ncol:ncol + wdt], 128, wdt,
                                rx.ap[:, 10 + tc:11 + tc], -128.0, 127.0,
                                x_tm[nti], src_c0=gcol)
                if g == 0 and tc == 0:
                    dbg_cp(24, rx.ap[:, 0:12], rx, 12)
                    dbg_cp(2560, bfv(n2bf_t[0])[:, 0:512], x_tm[n2bf_t[0]],
                           512)
                for hcg in range(13):
                    bank = pb[4]
                    nt = min(4, NC0 - hcg * 4)
                    grp_reads = [x_tm[n2bf_t[((hcg * 4 + q) * 128) // 4096]]
                                 for q in range(nt)] + [ident_b]
                    for q in range(nt):
                        hc = hcg * 4 + q
                        nti = n2bf_t[(hc * 128) // 4096]
                        ncol = (hc * 128) % 4096
                        k.do("tensor", "matmul",
                             bank.ap[:, q * 128:(q + 1) * 128],
                             bfv(nti)[:, ncol:ncol + 128], ident_b.ap[:, :],
                             start=True, stop=True,
                             reads=(grp_reads if q == nt - 1
                                    else [x_tm[nti], ident_b]),
                             writes=[bank],
                             sem="mm" if q == nt - 1 else None)
                    for q in range(nt):
                        hc = hcg * 4 + q
                        dst = n2T_t[hc // 16]
                        dcol = (hc % 16) * 256 + tc * 128
                        k.do("scalar", "activation",
                             out=bfv(dst)[:, dcol:dcol + 128],
                             in_=bank.ap[:, q * 128:(q + 1) * 128],
                             func=AF.Identity, reads=[bank],
                             writes=[x_tm[dst]], sem="aa")

            k.dma_batch("sync", "sx",
                        [(x_tm[xr_t[tc]].ap[:, :],
                          spill_x[G[tc] * 128:(G[tc] + 1) * 128, :])
                         for tc in range(2)],
                        writes=[x_tm[xr_t[0]], x_tm[xr_t[1]]],
                        extra=[v_spill])
            for et in range(4):
                ecols = slice(et * 512, (et + 1) * 512)
                items = []
                wr = []
                for ti in range(7):
                    h0 = ti * 1024
                    nb = min(8, NC0 - ti * 8)
                    if nb <= 0:
                        break
                    outv = bfv(dn_t[ti])[:, 0:nb * 512].rearrange(
                        "p (b c) -> p b c", c=512)
                    inv = wdT_q[h0:h0 + nb * 128, ecols].rearrange(
                        "(b p) c -> p b c", p=128)
                    items.append((outv, inv))
                    wr.append(x_tm[dn_t[ti]])
                k.dma_batch("sync", "sh", items, writes=wr,
                            extra=[v_qcache])
                if g == 0 and et == 0:
                    dbg_cp(3584, bfv(n2T_t[0])[:, 0:512], x_tm[n2T_t[0]], 512)
                    dbg_cp(4096, bfv(dn_t[0])[:, 0:512], x_tm[dn_t[0]], 512)
                for tc in range(2):
                    bank = pb[2 + tc]
                    grp_reads = ([x_tm[t] for t in n2T_t]
                                 + [x_tm[t] for t in dn_t])
                    for hc in range(NC0):
                        lhs = bfv(n2T_t[hc // 16])[:, (hc % 16) * 256 +
                                                   tc * 128:
                                                   (hc % 16) * 256 +
                                                   (tc + 1) * 128]
                        rhs = bfv(dn_t[hc // 8])[:, (hc % 8) * 512:
                                                 (hc % 8) * 512 + 512]
                        k.do("tensor", "matmul", bank.ap[:, :], lhs, rhs,
                             start=(hc == 0), stop=(hc == NC0 - 1),
                             reads=(grp_reads if hc == NC0 - 1
                                    else [x_tm[n2T_t[hc // 16]],
                                          x_tm[dn_t[hc // 8]]]),
                             writes=[bank],
                             sem="mm" if hc == NC0 - 1 else None)
                    xr = x_tm[xr_t[tc]]
                    if g == 0 and et == 0 and tc == 0:
                        dbg_cp(3072, bank.ap[:, :], bank, 512, psum=True)
                    k.do("vector", "scalar_tensor_tensor",
                         xr.ap[:, ecols], bank.ap[:, :],
                         rx.ap[:, 6 + tc:7 + tc], xr.ap[:, ecols],
                         ALU.mult, ALU.add, reads=[bank, rx], writes=[xr],
                         sem="vv")
            _layernorm(k, [x_tm[xr_t[0]], x_tm[xr_t[1]]], msc, t_a, sq)
            k.dma_batch("gpsimd", "so",
                        [(spill_x[G[tc] * 128:(G[tc] + 1) * 128, :],
                          x_tm[xr_t[tc]].ap[:, :]) for tc in range(2)],
                        reads=[x_tm[xr_t[0]], x_tm[xr_t[1]]])

        k.dma_batch("sync", "sx",
                    [(x_tm[i].ap[:, :], spill_x[i * 128:(i + 1) * 128, :])
                     for i in range(NTC)], writes=x_tm,
                    extra=[("so", k.val("so"))])


def _stage_l1_attn(cfg):
    k = cfg["k"]; nc = cfg["nc"]; W = cfg["W"]; C = cfg["C"]
    x_tm = cfg["x_tm"]; pb = cfg["pb"]; co = cfg["consts"]
    ident_b = co["ident_b"]; ident_f = co["ident_f"]
    flag = co["flag"]; flagi = co["flagi"]
    xt_h = cfg["xt_h"]; xt_l = cfg["xt_l"]; spill_x = cfg["spill_x"]
    cc1_in = cfg["cc1_in"]; cc1_out = cfg["cc1_out"]; PAIRS = cfg["PAIRS"]

    with contextlib.ExitStack() as st:
        def sbuf(name, shape, dt=F32):
            return Res(st.enter_context(nc.sbuf_tensor(name, shape, dt)))

        wld = sbuf("a1_wld", [128, 1024])
        w_h = sbuf("a1_wh", [128, D], BF16)
        w_l = sbuf("a1_wl", [128, D], BF16)
        xtb2 = [sbuf(f"a1_xtb{i}", [128, 2 * 512], BF16) for i in range(2)]
        g1 = sbuf("a1_g1", [128, T])
        kbuf = sbuf("a1_kbuf", [128, T])
        vbuf = sbuf("a1_vbuf", [128, T])
        qbuf = sbuf("a1_qbuf", [128, T])
        obuf = sbuf("a1_obuf", [128, T])
        onT_h = sbuf("a1_onTh", [128, T], BF16)
        onT_l = sbuf("a1_onTl", [128, T], BF16)
        cosb = sbuf("a1_cosb", [128, 512])
        sinb = sbuf("a1_sinb", [128, 512])
        mrot = sbuf("a1_mrot", [128, 128])
        dmk = sbuf("a1_dmk", [128, 128])
        nwj = sbuf("a1_nwj", [128, 128])
        deck = sbuf("a1_deck", [128, NH])
        decq = sbuf("a1_decq", [128, NH])
        s0p = sbuf("a1_s0p", [128, 128])
        s0c = [sbuf(f"a1_s0c{i}", [128, 128]) for i in range(2)]
        sct = [sbuf(f"a1_sct{i}", [128, 128]) for i in range(2)]
        t1 = sbuf("a1_t1", [128, 512])
        t2 = sbuf("a1_t2", [128, 512])
        colt = sbuf("a1_colt", [128, 4])
        msc = sbuf("a1_msc", [128, 2 * NTC])
        sq = sbuf("a1_sq", [128, 1])
        k.fence()

        k.dma("sync", deck.ap[:, :], C["deck"], "sw", writes=[deck])
        k.dma("sync", decq.ap[:, :], C["decq"], "sw", writes=[decq])
        k.dma("sync", mrot.ap[:, :], C["mrot"], "sw", writes=[mrot])

        loglam = [-(2.0 ** (-8.0 * (j + 1.0) / NH)) for j in range(NH)]
        lamC = [float(np.exp(loglam[j] * CH)) for j in range(NH)]

        xt_il = cfg["xt_il"]
        for tc in range(NTC):
            _split(k, x_tm[tc], w_h, w_l)
            for dcg in range(8):
                bank = pb[4]
                for qq in range(4):
                    dc = dcg * 2 + qq // 2
                    wsrc = w_h if qq % 2 == 0 else w_l
                    k.do("tensor", "matmul",
                         bank.ap[:, qq * 128:(qq + 1) * 128],
                         wsrc.ap[:, dc * 128:(dc + 1) * 128],
                         ident_b.ap[:, :], start=True, stop=True,
                         reads=[w_h, w_l, ident_b], writes=[bank],
                         sem="mm" if qq == 3 else None)
                k.do("scalar", "activation",
                     out=t1.ap[:, 0:256].bitcast(BF16),
                     in_=bank.ap[:, :], func=AF.Identity, reads=[bank],
                     writes=[t1], sem="aa")
                dst = xt_il[dcg * 512:(dcg + 1) * 512,
                            tc * 128:(tc + 1) * 128].rearrange(
                    "(b p) c -> p b c", p=128)
                k.dma("gpsimd", dst,
                      t1.ap[:, 0:256].bitcast(BF16).rearrange(
                          "p (b c) -> p b c", c=128), "so", reads=[t1])
        v_xt = ("so", k.val("so"))

        def load_xtb(tau, quarter, par):
            xtb = xtb2[par]
            c0 = tau * 512
            r0 = quarter * 256
            k.dma_batch("sync" if par == 0 else "scalar",
                        "sh" if par == 0 else "sh2",
                        [(xtb.ap[:, :].rearrange("p (b c) -> p b c", c=512),
                          xt_il[r0:r0 + 256, c0:c0 + 512].rearrange(
                              "(b p) c -> p b c", p=128))],
                        writes=[xtb], extra=[v_xt])

        def load_wcol(wap, j):
            for half in range(2):
                sl = wap[half * 1024:(half + 1) * 1024,
                         j * 128:(j + 1) * 128].rearrange(
                    "(dc p) c -> p dc c", p=128)
                k.dma("sync", wld.ap[:, 0:1024].rearrange(
                    "p (dc c) -> p dc c", c=128), sl, "sw", writes=[wld])
                _split(k, wld, w_h, w_l, cols=slice(half * 1024,
                                                    (half + 1) * 1024),
                       src_cols=slice(0, 1024))
            return

        def proj_pass(dst, silu, bias_dq=False):
            for tau in range(4):
                bank = pb[tau % 2]
                first = True
                for quarter in range(16):
                    par = (tau * 16 + quarter) % 2
                    load_xtb(tau, quarter, par)
                    xtb = xtb2[par]
                    dc = quarter
                    lh = w_h.ap[:, dc * 128:(dc + 1) * 128]
                    ll = w_l.ap[:, dc * 128:(dc + 1) * 128]
                    rh = xtb.ap[:, 0:512]
                    rl = xtb.ap[:, 512:1024]
                    last = (quarter == 15)
                    k.do("tensor", "matmul", bank.ap[:, :], lh, rh,
                         start=first, stop=False, reads=[w_h, xtb],
                         writes=[bank])
                    first = False
                    k.do("tensor", "matmul", bank.ap[:, :], lh, rl,
                         start=False, stop=False, reads=[w_h, xtb],
                         writes=[bank])
                    k.do("tensor", "matmul", bank.ap[:, :], ll, rh,
                         start=False, stop=last,
                         reads=[w_h, w_l, xtb], writes=[bank], sem="mm")
                cols = slice(tau * 512, (tau + 1) * 512)
                k.do("scalar", "activation", out=dst.ap[:, cols],
                     in_=bank.ap[:, :], func=AF.Silu if silu else AF.Identity,
                     reads=[bank], writes=[dst], sem="aa")

        def rotate(dst):
            for tau in range(4):
                cols = slice(tau * 512, (tau + 1) * 512)
                k.dma_batch("sync", "sx",
                            [(cosb.ap[:, :], C["cs_t"][:, cols]),
                             (sinb.ap[:, :], C["sc_t"][:, cols])],
                            writes=[cosb, sinb])
                k.do("vector", "tensor_mul", t1.ap[0:64, :],
                     dst.ap[0:64, cols], cosb.ap[0:64, :],
                     reads=[dst, cosb], writes=[t1], sem="vv")
                k.do("vector", "tensor_mul", t1.ap[64:128, :],
                     dst.ap[64:128, cols], cosb.ap[64:128, :],
                     reads=[dst, cosb], writes=[t1], sem="vv")
                k.do("vector", "tensor_mul", t2.ap[0:64, :],
                     dst.ap[0:64, cols], sinb.ap[0:64, :],
                     reads=[dst, sinb], writes=[t2], sem="vv")
                k.do("vector", "tensor_mul", t2.ap[64:128, :],
                     dst.ap[64:128, cols], sinb.ap[64:128, :],
                     reads=[dst, sinb], writes=[t2], sem="vv")
                bank = pb[2]
                k.do("tensor", "matmul", bank.ap[0:64, :],
                     mrot.ap[:, 0:64], t1.ap[:, :], start=True, stop=True,
                     reads=[mrot, t1], writes=[bank], sem="mm")
                k.do("tensor", "matmul", bank.ap[64:128, :],
                     mrot.ap[:, 64:128], t2.ap[:, :], start=True, stop=True,
                     tile_position=(0, 64), reads=[mrot, t2], writes=[bank],
                     sem="mm")
                k.do("scalar", "activation", out=dst.ap[:, cols],
                     in_=bank.ap[:, :], func=AF.Identity, reads=[bank],
                     writes=[dst], sem="aa")

        load_wcol(W["l1_wg1"], 0)
        for tau in range(4):
            bank = pb[2]
            first = True
            for quarter in range(16):
                par = (tau * 16 + quarter) % 2
                load_xtb(tau, quarter, par)
                xtb = xtb2[par]
                dc = quarter
                rh = xtb.ap[:, 0:512]
                rl = xtb.ap[:, 512:1024]
                last = (quarter == 15)
                k.do("tensor", "matmul", bank.ap[:, :],
                     w_h.ap[:, dc * 128:(dc + 1) * 128], rh,
                     start=first, stop=False,
                     reads=[w_h, xtb], writes=[bank])
                first = False
                k.do("tensor", "matmul", bank.ap[:, :],
                     w_h.ap[:, dc * 128:(dc + 1) * 128], rl,
                     start=False, stop=False,
                     reads=[w_h, xtb], writes=[bank])
                k.do("tensor", "matmul", bank.ap[:, :],
                     w_l.ap[:, dc * 128:(dc + 1) * 128], rh,
                     start=False, stop=last,
                     reads=[w_h, w_l, xtb], writes=[bank], sem="mm")
            k.do("scalar", "activation", out=g1.ap[:, tau * 512:(tau + 1) * 512],
                 in_=bank.ap[:, :], func=AF.Identity, reads=[bank],
                 writes=[g1], sem="aa")

        for j in range(NH):
            jc = slice(j * 128, (j + 1) * 128)
            load_wcol(W["l1_wk"], j)
            proj_pass(kbuf, silu=True)
            rotate(kbuf)
            load_wcol(W["l1_wv"], j)
            proj_pass(qbuf, silu=False)
            for c in range(16):
                bank = pb[4]
                k.do("tensor", "transpose", bank.ap[:, 0:128],
                     qbuf.ap[:, c * 128:(c + 1) * 128], ident_f.ap[:, :],
                     reads=[qbuf, ident_f], writes=[bank], sem="mm")
                k.do("scalar", "activation",
                     out=vbuf.ap[:, c * 128:(c + 1) * 128],
                     in_=bank.ap[:, 0:128], func=AF.Identity, reads=[bank],
                     writes=[vbuf], sem="aa")
            load_wcol(W["l1_wq"], j)
            proj_pass(qbuf, silu=True)
            rotate(qbuf)
            k.dma_batch("sync", "sx",
                        [(dmk.ap[:, :], C["dmask"][:, jc]),
                         (nwj.ap[:, :], C["nw_rep"][:, jc])],
                        writes=[dmk, nwj])
            for c in range(16):
                cc = slice(c * 128, (c + 1) * 128)
                pa = c % 2
                bs = pb[2 + 4 * pa]
                bo_ = pb[3 + 4 * pa]
                bt = pb[4 + pa]
                k.do("tensor", "matmul", bs.ap[:, 0:128],
                     kbuf.ap[:, cc], qbuf.ap[:, cc], start=True, stop=True,
                     reads=[kbuf, qbuf], writes=[bs], sem="mm")
                k.do("vector", "tensor_mul", sct[pa].ap[:, :],
                     bs.ap[:, 0:128], dmk.ap[:, :], reads=[bs, dmk],
                     writes=[sct[pa]], sem="vv")
                k.do("tensor", "transpose", bt.ap[:, 0:128],
                     kbuf.ap[:, cc], ident_f.ap[:, :],
                     reads=[kbuf, ident_f], writes=[bt], sem="mm")
                k.do("vector", "tensor_scalar", kbuf.ap[:, cc],
                     bt.ap[:, 0:128], deck.ap[:, j:j + 1], None, ALU.mult,
                     reads=[bt, deck], writes=[kbuf], sem="vv")
                k.do("tensor", "matmul", bo_.ap[:, 0:128], sct[pa].ap[:, :],
                     vbuf.ap[:, cc], start=True, stop=True,
                     reads=[sct[pa], vbuf], writes=[bo_], sem="mm")
                k.do("scalar", "activation", out=obuf.ap[:, cc],
                     in_=bo_.ap[:, 0:128], func=AF.Identity,
                     reads=[bo_], writes=[obuf], sem="aa")
                k.do("tensor", "matmul", bt.ap[:, 128:256],
                     kbuf.ap[:, cc], vbuf.ap[:, cc], start=True, stop=True,
                     reads=[kbuf, vbuf], writes=[bt], sem="mm")
                if c == 0:
                    k.do("vector", "tensor_scalar", vbuf.ap[:, cc],
                         bt.ap[:, 128:256], 1.0, None, ALU.mult,
                         reads=[bt], writes=[vbuf], sem="vv")
                else:
                    k.do("vector", "scalar_tensor_tensor", vbuf.ap[:, cc],
                         vbuf.ap[:, (c - 1) * 128:c * 128], lamC[j],
                         bt.ap[:, 128:256], ALU.mult, ALU.add,
                         reads=[vbuf, bt], writes=[vbuf], sem="vv")
            k.do("gpsimd", "tensor_scalar", t1.ap[:, 0:128],
                 vbuf.ap[:, 15 * 128:16 * 128], flagi.ap[:, 0:1], None,
                 ALU.mult, reads=[vbuf, flagi], writes=[t1], sem="gg")
            vz = k.dma("gpsimd", cc1_in[j][:, :], t1.ap[:, 0:128], "sc",
                       reads=[t1])
            k.do("gpsimd", "collective_compute", "AllReduce", ALU.add,
                 PAIRS, [cc1_in[j][:, :]], [cc1_out[j][:, :]], sem="cc",
                 extra=[vz])
            k.dma("gpsimd", s0p.ap[:, :], cc1_out[j][:, :], "sc",
                  writes=[s0p], extra=[("cc", k.val("cc"))])
            k.do("gpsimd", "tensor_scalar", s0p.ap[:, :], s0p.ap[:, :],
                 flag.ap[:, 0:1], None, ALU.mult, reads=[flag], writes=[s0p],
                 sem="gg")
            for c in range(16):
                cc = slice(c * 128, (c + 1) * 128)
                pa = c % 2
                bo_ = pb[3 + 4 * pa]
                lam_p = float(np.exp(loglam[j] * CH * c))
                k.do("vector", "tensor_scalar", s0c[pa].ap[:, :],
                     s0p.ap[:, :], lam_p, None, ALU.mult, reads=[s0p],
                     writes=[s0c[pa]], sem="vv")
                if c == 0:
                    k.do("tensor", "matmul", bo_.ap[:, 0:128],
                         qbuf.ap[:, cc], s0c[pa].ap[:, :], start=True,
                         stop=True, reads=[qbuf, s0c[pa]], writes=[bo_],
                         sem="mm")
                else:
                    k.do("tensor", "matmul", bo_.ap[:, 0:128],
                         qbuf.ap[:, cc], vbuf.ap[:, (c - 1) * 128:c * 128],
                         start=True, stop=False, reads=[qbuf, vbuf],
                         writes=[bo_])
                    k.do("tensor", "matmul", bo_.ap[:, 0:128],
                         qbuf.ap[:, cc], s0c[pa].ap[:, :], start=False,
                         stop=True, reads=[qbuf, vbuf, s0c[pa]],
                         writes=[bo_], sem="mm")
                k.do("vector", "scalar_tensor_tensor", obuf.ap[:, cc],
                     bo_.ap[:, 0:128], decq.ap[:, j:j + 1],
                     obuf.ap[:, cc], ALU.mult, ALU.add,
                     reads=[bo_, decq], writes=[obuf], sem="vv")
            slot = 0
            k.dma("sync", wld.ap[:, 0:128],
                  W["l1_wg2"][:, jc], "sw", writes=[wld])
            for c in range(16):
                cc = slice(c * 128, (c + 1) * 128)
                k.do("tensor", "matmul", pb[5].ap[:, 0:128],
                     g1.ap[:, cc], wld.ap[:, 0:128], start=True, stop=True,
                     reads=[g1, wld], writes=[pb[5]], sem="mm")
                k.do("scalar", "activation", out=t1.ap[:, 0:128],
                     in_=pb[5].ap[:, 0:128], func=AF.Sigmoid,
                     reads=[pb[5]], writes=[t1], sem="aa")
                k.do("vector", "tensor_mul", t1.ap[:, 0:128],
                     obuf.ap[:, cc], t1.ap[:, 0:128], reads=[obuf, t1],
                     writes=[t1], sem="vv")
                k.do("vector", "tensor_mul", t2.ap[:, 0:128],
                     t1.ap[:, 0:128], t1.ap[:, 0:128], reads=[t1],
                     writes=[t2], sem="vv")
                k.do("vector", "tensor_reduce", colt.ap[:, 0:1],
                     t2.ap[:, 0:128], axis=AXX, op=ALU.add, reads=[t2],
                     writes=[colt], sem="vv")
                k.do("vector", "tensor_scalar", colt.ap[:, 1:2],
                     colt.ap[:, 0:1], 1.0 / 128.0, 1e-6, ALU.mult, ALU.add,
                     reads=[colt], writes=[colt], sem="vv")
                k.do("scalar", "activation", out=colt.ap[:, 2:3],
                     in_=colt.ap[:, 1:2], func=AF.Sqrt, reads=[colt],
                     writes=[colt], sem="aa")
                k.do("vector", "reciprocal", colt.ap[:, 3:4],
                     colt.ap[:, 2:3], reads=[colt], writes=[colt], sem="vv")
                k.do("vector", "tensor_scalar", t1.ap[:, 0:128],
                     t1.ap[:, 0:128], colt.ap[:, 3:4], None, ALU.mult,
                     reads=[colt], writes=[t1], sem="vv")
                k.do("vector", "tensor_mul", t1.ap[:, 0:128],
                     t1.ap[:, 0:128], nwj.ap[:, :], reads=[nwj],
                     writes=[t1], sem="vv")
                k.do("scalar", "activation",
                     out=t2.ap[:, 0:64].bitcast(BF16),
                     in_=t1.ap[:, 0:128], func=AF.Identity, reads=[t1],
                     writes=[t2], sem="aa")
                k.do("vector", "tensor_sub",
                     t2.ap[:, 64:128].bitcast(BF16),
                     t1.ap[:, 0:128], t2.ap[:, 0:64].bitcast(BF16),
                     reads=[t1], writes=[t2], sem="vv")
                for hl, dstb in ((0, onT_h), (1, onT_l)):
                    srcap = (t2.ap[:, 0:64].bitcast(BF16) if hl == 0
                             else t2.ap[:, 64:128].bitcast(BF16))
                    k.do("tensor", "matmul", pb[4].ap[:, 0:128],
                         srcap, ident_b.ap[:, :], start=True, stop=True,
                         reads=[t2, ident_b], writes=[pb[4]], sem="mm")
                    k.do("scalar", "activation",
                         out=dstb.ap[:, c * 128:(c + 1) * 128],
                         in_=pb[4].ap[:, 0:128], func=AF.Identity,
                         reads=[pb[4]], writes=[dstb], sem="aa")
            if True:
                for s2 in range(1):
                    jo = j
                    k.dma("sync", wld.ap[:, 0:1024],
                          W["l1_wo"][jo * 128:(jo + 1) * 128, 0:1024], "sw",
                          writes=[wld])
                    _split(k, wld, w_h, w_l, cols=slice(0, 1024),
                           src_cols=slice(0, 1024))
                    k.dma("sync", wld.ap[:, 0:1024],
                          W["l1_wo"][jo * 128:(jo + 1) * 128, 1024:2048],
                          "sw", writes=[wld])
                    _split(k, wld, w_h, w_l, cols=slice(1024, 2048),
                           src_cols=slice(0, 1024))
                    for et in range(4):
                        ecols = slice(et * 512, (et + 1) * 512)
                        for tc in range(NTC):
                            bank = pb[6 + tc % 2]
                            ch = onT_h.ap[:, tc * 128:(tc + 1) * 128]
                            cl = onT_l.ap[:, tc * 128:(tc + 1) * 128]
                            rh = w_h.ap[:, ecols]
                            rl = w_l.ap[:, ecols]
                            k.do("tensor", "matmul", bank.ap[:, :], ch, rh,
                                 start=True, stop=False,
                                 reads=[onT_h, w_h], writes=[bank])
                            k.do("tensor", "matmul", bank.ap[:, :], ch, rl,
                                 start=False, stop=False,
                                 reads=[onT_h, w_l], writes=[bank])
                            k.do("tensor", "matmul", bank.ap[:, :], cl, rh,
                                 start=False, stop=True,
                                 reads=[onT_h, onT_l, w_h, w_l],
                                 writes=[bank], sem="mm")
                            k.do("vector", "tensor_add",
                                 x_tm[tc].ap[:, ecols], x_tm[tc].ap[:, ecols],
                                 bank.ap[:, :], reads=[bank],
                                 writes=[x_tm[tc]], sem="vv")

        _layernorm(k, x_tm, msc, qbuf, sq)


def _stage_l1_ffn(cfg):
    k = cfg["k"]; nc = cfg["nc"]; W = cfg["W"]; C = cfg["C"]
    x_tm = cfg["x_tm"]; pb = cfg["pb"]; co = cfg["consts"]
    ident_b = co["ident_b"]; ones_f = co["ones_f"]
    xt_h = cfg["xt_h"]; xt_l = cfg["xt_l"]; spill_x = cfg["spill_x"]
    w1T_q = cfg["w1T_q"]; w3T_q = cfg["w3T_q"]; w2T_q = cfg["w2T_q"]
    HT = [(i * 512, 512) for i in range(10)] + [(5120, 384)]

    with contextlib.ExitStack() as st:
        def sbuf(name, shape, dt=F32):
            return Res(st.enter_context(nc.sbuf_tensor(name, shape, dt)))

        w_f = sbuf("f1_wf", [128, D])
        wq_b = sbuf("f1_wqb", [128, D], BF16)
        tx_b = sbuf("f1_txb", [128, D], BF16)
        t_a = sbuf("f1_ta", [128, D])
        t_b = sbuf("f1_tb", [128, D])
        i_t = sbuf("f1_it", [128, D], mybir.dt.int32)
        scq = sbuf("f1_scq", [128, 8])
        acc = sbuf("f1_acc", [128, 2])
        sm = sbuf("f1_sm", [1, 8])
        msc = sbuf("f1_msc", [128, 8])
        sq = sbuf("f1_sq", [128, 1])
        k.fence()

        def bfv(ti):
            return x_tm[ti].ap.bitcast(BF16)

        xt_il = cfg["xt_il"]
        for tc in range(NTC):
            _split(k, x_tm[tc], wq_b, tx_b)
            for dcg in range(8):
                bank = pb[4]
                for qq in range(4):
                    dc = dcg * 2 + qq // 2
                    wsrc = wq_b if qq % 2 == 0 else tx_b
                    k.do("tensor", "matmul",
                         bank.ap[:, qq * 128:(qq + 1) * 128],
                         wsrc.ap[:, dc * 128:(dc + 1) * 128],
                         ident_b.ap[:, :], start=True, stop=True,
                         reads=[wq_b, tx_b, ident_b], writes=[bank],
                         sem="mm" if qq == 3 else None)
                k.do("scalar", "activation",
                     out=t_a.ap[:, 0:256].bitcast(BF16),
                     in_=bank.ap[:, :], func=AF.Identity, reads=[bank],
                     writes=[t_a], sem="aa")
                dst = xt_il[dcg * 512:(dcg + 1) * 512,
                            tc * 128:(tc + 1) * 128].rearrange(
                    "(b p) c -> p b c", p=128)
                k.dma("gpsimd", dst,
                      t_a.ap[:, 0:256].bitcast(BF16).rearrange(
                          "p (b c) -> p b c", c=128), "so", reads=[t_a])
        k.dma_batch("gpsimd", "so",
                    [(spill_x[i * 128:(i + 1) * 128, :], x_tm[i].ap[:, :])
                     for i in range(NTC)], reads=x_tm)
        v_spill = ("so", k.val("so"))

        def wchunks(nm):
            if nm in ("l1_w1", "l1_w3"):
                for r0 in range(0, I1, 128):
                    nr = min(128, I1 - r0)
                    yield W[nm][r0:r0 + nr, :], nr, D
            else:
                for r0 in range(0, D, 128):
                    for c0 in range(0, I1, 2048):
                        w = min(2048, I1 - c0)
                        yield W[nm][r0:r0 + 128, c0:c0 + w], 128, w

        for mi, nm in enumerate(["l1_w1", "l1_w3", "l1_w2"]):
            k.do("vector", "memset", acc.ap[:, 0:2], 0.0, writes=[acc],
                 sem="vv")
            for sl, nr, wcols in wchunks(nm):
                k.dma("sync", w_f.ap[0:nr, 0:wcols], sl, "sw", writes=[w_f])
                k.do("scalar", "activation", out=t_a.ap[0:nr, 0:wcols],
                     in_=w_f.ap[0:nr, 0:wcols], func=AF.Abs, reads=[w_f],
                     writes=[t_a], sem="aa")
                k.do("vector", "tensor_reduce", acc.ap[0:nr, 1:2],
                     t_a.ap[0:nr, 0:wcols], axis=AXX, op=ALU.add,
                     reads=[t_a], writes=[acc], sem="vv")
                k.do("vector", "tensor_add", acc.ap[0:nr, 0:1],
                     acc.ap[0:nr, 0:1], acc.ap[0:nr, 1:2], reads=[acc],
                     writes=[acc], sem="vv")
            k.do("tensor", "matmul", pb[5].ap[0:1, 0:1], acc.ap[:, 0:1],
                 ones_f.ap[:, 0:1], start=True, stop=True,
                 reads=[acc, ones_f], writes=[pb[5]], sem="mm")
            k.do("vector", "tensor_scalar", sm.ap[0:1, 0:1],
                 pb[5].ap[0:1, 0:1], 1.0 / (I1 * D), None, ALU.mult,
                 reads=[pb[5]], writes=[sm], sem="vv")
            k.do("vector", "reciprocal", sm.ap[0:1, 1:2], sm.ap[0:1, 0:1],
                 reads=[sm], writes=[sm], sem="vv")
            k.do("tensor", "matmul", pb[5].ap[:, 0:2], ones_f.ap[0:1, 0:128],
                 sm.ap[0:1, 0:2], start=True, stop=True, reads=[ones_f, sm],
                 writes=[pb[5]], sem="mm")
            k.do("vector", "tensor_copy", scq.ap[:, 2 * mi:2 * mi + 1],
                 pb[5].ap[:, 1:2], reads=[pb[5]], writes=[scq], sem="vv")
            k.do("vector", "tensor_copy", scq.ap[:, 2 * mi + 1:2 * mi + 2],
                 pb[5].ap[:, 0:1], reads=[pb[5]], writes=[scq], sem="vv")

        def quant_chain(src, dst_ap, nr, wcols, sc_col, dst_res, src_c0=0):
            k.do("vector", "tensor_scalar", t_a.ap[0:nr, 0:wcols],
                 src.ap[0:nr, src_c0:src_c0 + wcols], sc_col[0:nr, :], 0.5,
                 ALU.mult, ALU.add, reads=[src, scq], writes=[t_a], sem="vv")
            k.do("vector", "tensor_scalar_add", i_t.ap[0:nr, 0:wcols],
                 t_a.ap[0:nr, 0:wcols], 0.0, reads=[t_a], writes=[i_t],
                 sem="vv")
            k.do("vector", "tensor_tensor", t_b.ap[0:nr, 0:wcols],
                 i_t.ap[0:nr, 0:wcols], t_a.ap[0:nr, 0:wcols], ALU.is_gt,
                 reads=[i_t, t_a], writes=[t_b], sem="vv")
            k.do("vector", "tensor_sub", t_a.ap[0:nr, 0:wcols],
                 i_t.ap[0:nr, 0:wcols], t_b.ap[0:nr, 0:wcols],
                 reads=[i_t, t_b], writes=[t_a], sem="vv")
            k.do("vector", "tensor_scalar", dst_ap, t_a.ap[0:nr, 0:wcols],
                 1.0, -1.0, ALU.min, ALU.max, reads=[t_a],
                 writes=[dst_res], sem="vv")

        for nm, dram, sc_i in [("l1_w1", w1T_q, 0), ("l1_w3", w3T_q, 2)]:
            for ci, r0 in enumerate(range(0, I1P, 128)):
                nr = min(128, I1 - r0) if r0 < I1 else 0
                if nr < 128:
                    k.do("vector", "memset", wq_b.ap[:, :], 0.0,
                         writes=[wq_b], sem="vv")
                if nr > 0:
                    k.dma("sync", w_f.ap[0:nr, :], W[nm][r0:r0 + nr, :], "sw",
                          writes=[w_f])
                    quant_chain(w_f, wq_b.ap[0:nr, :], nr, D,
                                scq.ap[:, sc_i:sc_i + 1], wq_b)
                _transpose16(k, pb, ident_b, wq_b, tx_b, n=16)
                out = dram[0:D, ci * 128:(ci + 1) * 128].rearrange(
                    "(dc p) c -> p dc c", p=128)
                in_ = tx_b.ap[:, :].rearrange("p (dc c) -> p dc c", c=128)
                k.dma("gpsimd", out, in_, "sq", reads=[tx_b])
        for ci, r0 in enumerate(range(0, D, 128)):
            for pi, c0 in enumerate(range(0, I1P, 2048)):
                wreal = max(0, min(2048, I1 - c0))
                wpad = min(2048, I1P - c0)
                ntile = wpad // 128
                if wreal < wpad:
                    k.do("vector", "memset", wq_b.ap[:, wreal:wpad], 0.0,
                         writes=[wq_b], sem="vv")
                k.dma("sync", w_f.ap[:, 0:wreal],
                      W["l1_w2"][r0:r0 + 128, c0:c0 + wreal], "sw",
                      writes=[w_f])
                quant_chain(w_f, wq_b.ap[:, 0:wreal], 128, wreal,
                            scq.ap[:, 4:5], wq_b)
                _transpose16(k, pb, ident_b, wq_b, tx_b, n=ntile)
                out = w2T_q[c0:c0 + ntile * 128,
                            ci * 128:(ci + 1) * 128].rearrange(
                    "(dc p) c -> p dc c", p=128)
                in_ = tx_b.ap[:, 0:ntile * 128].rearrange(
                    "p (dc c) -> p dc c", c=128)
                k.dma("gpsimd", out, in_, "sq", reads=[tx_b])
        v_qcache = ("sq", k.val("sq"))

        for g in range(8):
            G = [2 * g, 2 * g + 1]
            scr = [i for i in range(NTC) if i not in G]
            xT_t = [scr[0], scr[1]]
            w1_t = [scr[2], scr[3]]
            w3_t = [scr[4], scr[5]]
            gT_t = scr[6:12]
            w2_t = [scr[12], scr[13], G[0], G[1], scr[2], scr[3]]
            xr_t = [scr[4], scr[5]]

            items = []
            for tc in range(2):
                outv = bfv(xT_t[tc]).rearrange("p (b c) -> p b c", c=128)
                inv = xt_il[:, G[tc] * 128:(G[tc] + 1) * 128].rearrange(
                    "(b p) c -> p b c", p=128)
                items.append((outv, inv))
            k.dma_batch("sync", "sx", items,
                        writes=[x_tm[xT_t[0]], x_tm[xT_t[1]]],
                        extra=[v_spill])

            for ht, (h0, wdt) in enumerate(HT):
                items = []
                for dram, tiles in ((w1T_q, w1_t), (w3T_q, w3_t)):
                    for half in range(2):
                        outv = bfv(tiles[half])[:, 0:8 * 512].rearrange(
                            "p (dc c) -> p dc c", c=512)[:, :, 0:wdt]
                        inv = dram[half * 1024:(half + 1) * 1024,
                                   h0:h0 + wdt].rearrange(
                            "(dc p) c -> p dc c", p=128)
                        items.append((outv, inv))
                k.dma_batch("sync", "sh", items,
                            writes=[x_tm[w1_t[0]], x_tm[w1_t[1]],
                                    x_tm[w3_t[0]], x_tm[w3_t[1]]],
                            extra=[v_qcache])
                for tc in range(2):
                    for pi, tiles in enumerate((w1_t, w3_t)):
                        bank = pb[pi]
                        grp_reads = [x_tm[xT_t[0]], x_tm[xT_t[1]],
                                     x_tm[tiles[0]], x_tm[tiles[1]]]
                        first = True
                        for dc in range(16):
                            til = tiles[dc // 8]
                            rhs = bfv(til)[:, (dc % 8) * 512:
                                           (dc % 8) * 512 + wdt]
                            for hl in range(2):
                                lhs = bfv(xT_t[tc])[
                                    :, (2 * dc + hl) * 128:
                                    (2 * dc + hl + 1) * 128]
                                last = (dc == 15 and hl == 1)
                                k.do("tensor", "matmul", bank.ap[:, 0:wdt],
                                     lhs, rhs, start=first, stop=last,
                                     reads=(grp_reads if last else
                                            [x_tm[xT_t[tc]], x_tm[til]]),
                                     writes=[bank],
                                     sem="mm" if last else None)
                                first = False
                    k.do("scalar", "activation", out=t_a.ap[:, 0:wdt],
                         in_=pb[0].ap[:, 0:wdt], func=AF.Silu,
                         scale=scq.ap[:, 1:2], reads=[pb[0], scq],
                         writes=[t_a], sem="aa")
                    k.do("vector", "tensor_scalar", t_b.ap[:, 0:wdt],
                         pb[1].ap[:, 0:wdt], scq.ap[:, 3:4], None, ALU.mult,
                         reads=[pb[1], scq], writes=[t_b], sem="vv")
                    k.do("vector", "tensor_mul", t_a.ap[:, 0:wdt],
                         t_a.ap[:, 0:wdt], t_b.ap[:, 0:wdt], reads=[t_b],
                         writes=[t_a], sem="vv")
                    k.do("scalar", "activation",
                         out=t_b.ap[:, 0:wdt // 2].bitcast(BF16)[:, 0:wdt],
                         in_=t_a.ap[:, 0:wdt], func=AF.Identity, reads=[t_a],
                         writes=[t_b], sem="aa")
                    k.do("vector", "tensor_sub",
                         t_b.ap[:, 1024:1024 + wdt // 2].bitcast(BF16)[:, 0:wdt],
                         t_a.ap[:, 0:wdt],
                         t_b.ap[:, 0:wdt // 2].bitcast(BF16)[:, 0:wdt],
                         reads=[t_a], writes=[t_b], sem="vv")
                    nt_all = wdt // 128
                    for hcg in range((nt_all + 3) // 4):
                        for hl in range(2):
                            bank = pb[4]
                            nt = min(4, nt_all - hcg * 4)
                            for qq in range(nt):
                                ii = hcg * 4 + qq
                                srcap = t_b.ap[:, hl * 1024:hl * 1024 +
                                               wdt // 2].bitcast(BF16)[
                                    :, ii * 128:(ii + 1) * 128]
                                k.do("tensor", "matmul",
                                     bank.ap[:, qq * 128:(qq + 1) * 128],
                                     srcap, ident_b.ap[:, :], start=True,
                                     stop=True, reads=[t_b, ident_b],
                                     writes=[bank],
                                     sem="mm" if qq == nt - 1 else None)
                            for qq in range(nt):
                                hc = (h0 // 128) + hcg * 4 + qq
                                dst = gT_t[hc // 8]
                                dcol = (hc % 8) * 512 + tc * 256 + hl * 128
                                k.do("scalar", "activation",
                                     out=bfv(dst)[:, dcol:dcol + 128],
                                     in_=bank.ap[:, qq * 128:(qq + 1) * 128],
                                     func=AF.Identity, reads=[bank],
                                     writes=[x_tm[dst]], sem="aa")

            k.dma_batch("sync", "sx",
                        [(x_tm[xr_t[tc]].ap[:, :],
                          spill_x[G[tc] * 128:(G[tc] + 1) * 128, :])
                         for tc in range(2)],
                        writes=[x_tm[xr_t[0]], x_tm[xr_t[1]]],
                        extra=[v_spill])
            for et in range(4):
                ecols = slice(et * 512, (et + 1) * 512)
                items = []
                wr = []
                for ti in range(6):
                    h0 = ti * 1024
                    nb = min(8, NC1 - ti * 8)
                    if nb <= 0:
                        break
                    outv = bfv(w2_t[ti])[:, 0:nb * 512].rearrange(
                        "p (b c) -> p b c", c=512)
                    inv = w2T_q[h0:h0 + nb * 128, ecols].rearrange(
                        "(b p) c -> p b c", p=128)
                    items.append((outv, inv))
                    wr.append(x_tm[w2_t[ti]])
                k.dma_batch("sync", "sh", items, writes=wr,
                            extra=[v_qcache])
                for tc in range(2):
                    bank = pb[2 + tc]
                    grp_reads = ([x_tm[t] for t in gT_t]
                                 + [x_tm[t] for t in w2_t[:6]])
                    first = True
                    for hc in range(NC1):
                        rhs = bfv(w2_t[hc // 8])[:, (hc % 8) * 512:
                                                 (hc % 8) * 512 + 512]
                        gti = gT_t[hc // 8]
                        base = (hc % 8) * 512 + tc * 256
                        for hl in range(2):
                            lhs = bfv(gti)[:, base + hl * 128:
                                           base + (hl + 1) * 128]
                            last = (hc == NC1 - 1 and hl == 1)
                            k.do("tensor", "matmul", bank.ap[:, :], lhs, rhs,
                                 start=first, stop=last,
                                 reads=(grp_reads if last else
                                        [x_tm[gti], x_tm[w2_t[hc // 8]]]),
                                 writes=[bank], sem="mm" if last else None)
                            first = False
                    xr = x_tm[xr_t[tc]]
                    k.do("vector", "scalar_tensor_tensor",
                         xr.ap[:, ecols], bank.ap[:, :], scq.ap[:, 5:6],
                         xr.ap[:, ecols], ALU.mult, ALU.add,
                         reads=[bank, scq], writes=[xr], sem="vv")
            _layernorm(k, [x_tm[xr_t[0]], x_tm[xr_t[1]]], msc, t_a, sq)
            k.dma_batch("gpsimd", "so",
                        [(spill_x[G[tc] * 128:(G[tc] + 1) * 128, :],
                          x_tm[xr_t[tc]].ap[:, :]) for tc in range(2)],
                        reads=[x_tm[xr_t[0]], x_tm[xr_t[1]]])

        k.dma_batch("sync", "sx",
                    [(x_tm[i].ap[:, :], spill_x[i * 128:(i + 1) * 128, :])
                     for i in range(NTC)], writes=x_tm,
                    extra=[("so", k.val("so"))])


def _make_consts(core_id):
    c = {}
    c["ident_b"] = np.eye(128, dtype=BF)
    c["ident_f"] = np.eye(128, dtype=np.float32)
    c["ones_f"] = np.ones((128, 512), dtype=np.float32)
    c["flag"] = np.full((128, 1), float(core_id % 2), np.float32)
    c["flagi"] = np.full((128, 1), 1.0 - float(core_id % 2), np.float32)
    t0 = (core_id % 2) * T
    pos = np.arange(t0, t0 + T, dtype=np.float64)
    j = np.arange(64, dtype=np.float64)
    theta = 10000.0 ** (-2.0 * j / 128.0)
    ang = pos[None, :] * theta[:, None]
    cos = np.cos(ang).astype(np.float32)
    sin = np.sin(ang).astype(np.float32)
    c["cs_t"] = np.concatenate([cos, sin], axis=0)
    c["sc_t"] = np.concatenate([sin, cos], axis=0)
    ey = np.eye(64, dtype=np.float32)
    m1 = np.concatenate([ey, -ey], axis=0)
    m2 = np.concatenate([ey, ey], axis=0)
    c["mrot"] = np.concatenate([m1, m2], axis=1)
    loglam = -(2.0 ** (-8.0 * (np.arange(NH, dtype=np.float64) + 1.0) / NH))
    i = np.arange(CH, dtype=np.float64)
    dm = np.zeros((128, NH * 128), np.float32)
    for h in range(NH):
        diff = i[None, :] - i[:, None]
        m = np.where(diff >= 0, np.exp(loglam[h] * diff), 0.0)
        dm[:, h * 128:(h + 1) * 128] = m.astype(np.float32)
    c["dmask"] = dm
    c["deck"] = np.exp(loglam[None, :] * (CH - 1.0 - i[:, None])).astype(np.float32)
    c["decq"] = np.exp(loglam[None, :] * (i[:, None] + 1.0)).astype(np.float32)
    return c


def kernel(**inputs):
    x = np.ascontiguousarray(np.asarray(inputs["L0_attn_x"], np.float32)
                             .reshape(4 * 4096, D))
    wmap = {
        "l0_wq": "L0_attn_Wq", "l0_wk": "L0_attn_Wk", "l0_wv": "L0_attn_Wv",
        "l0_wo": "L0_attn_Wo", "l0_wg": "L0_ffn_w_gate", "l0_wu": "L0_ffn_w_up",
        "l0_wd": "L0_ffn_w_down", "l1_wq": "L1_attn_Wq", "l1_wk": "L1_attn_Wk",
        "l1_wv": "L1_attn_Wv", "l1_wo": "L1_attn_Wo", "l1_wg1": "L1_attn_Wg1",
        "l1_wg2": "L1_attn_Wg2", "l1_w1": "L1_ffn_w1", "l1_w3": "L1_ffn_w3",
        "l1_w2": "L1_ffn_w2",
    }
    ws = {kk: np.ascontiguousarray(np.asarray(inputs[v], np.float32))
          for kk, v in wmap.items()}
    bq = np.asarray(inputs["L0_attn_bq"], np.float32)
    bk = np.asarray(inputs["L0_attn_bk"], np.float32)
    bv = np.asarray(inputs["L0_attn_bv"], np.float32)
    bo = np.asarray(inputs["L0_attn_bo"], np.float32)
    nw = np.asarray(inputs["L1_attn_norm_weight"], np.float32)

    nc = _build_program()

    in_maps = []
    for c in range(NCORES):
        xs = x[c * T:(c + 1) * T]
        m = dict(ws)
        m["x_in"] = xs
        m["xT_in"] = np.ascontiguousarray(xs.T)
        m.update(_make_consts(c))
        m["bq_sb"] = np.ascontiguousarray(bq.reshape(NH, 128).T)
        m["bk_sb"] = np.ascontiguousarray(bk.reshape(NH, 128).T)
        m["bv_sb"] = np.ascontiguousarray(bv.reshape(NH, 128).T)
        m["bo_rep"] = np.ascontiguousarray(np.broadcast_to(bo, (128, D)))
        m["nw_rep"] = np.ascontiguousarray(np.broadcast_to(nw, (128, D)))
        in_maps.append(m)

    res = run_bass_kernel_spmd(nc, in_maps, core_ids=list(range(NCORES)))
    out = np.concatenate([r["y_out"] for r in res.results], axis=0)
    return out.reshape(4, 4096, D)


# version 12 (final)
# speedup vs baseline: 1.0414x; 1.0414x over previous
import contextlib
import numpy as np
import ml_dtypes
import concourse.bass as bass
import concourse.mybir as mybir
from concourse.bass_utils import run_bass_kernel_spmd

F32 = mybir.dt.float32
BF16 = mybir.dt.bfloat16
AF = mybir.ActivationFunctionType
ALU = mybir.AluOpType
AXX = mybir.AxisListType.X
BF = ml_dtypes.bfloat16

NCORES = 8
T = 2048
D = 2048
NH = 16
HD = 128
NTC = T // 128
NDC = D // 128
I0 = 6464
I0P = 6528
NC0 = 51
I1 = 5461
I1P = 5504
NC1 = 43
CH = 128
SCALE0 = float(HD) ** -0.5

_N_STAGES = 99
_DBG = False

ENGS = ("sync", "scalar", "vector", "tensor", "gpsimd")


class Res:
    __slots__ = ("ap", "wdep", "rdeps")

    def __init__(self, ap=None):
        self.ap = ap
        self.wdep = None
        self.rdeps = []


class KB:
    def __init__(self, nc):
        self.nc = nc
        self.ops = {e: [] for e in ENGS}
        self.cnt = {}
        self.S = {}
        self.lastw = {}

    def op(self, eng, fn):
        self.ops[eng].append(fn)

    def inc(self, sem, kk=1):
        self.cnt[sem] = self.cnt.get(sem, 0) + kk
        return self.cnt[sem]

    def val(self, sem):
        return self.cnt.get(sem, 0)

    def wait_raw(self, eng, sem, v):
        if not v or v <= 0:
            return
        key = (eng, sem)
        if self.lastw.get(key, 0) >= v:
            return
        self.lastw[key] = v
        self.op(eng, lambda E, s=sem, vv=v: E.wait_ge(self.S[s], vv))

    OWN = {"tensor": "mm"}

    def waits(self, eng, deps):
        own = self.OWN.get(eng)
        best = {}
        for d in deps:
            if d is None:
                continue
            s, v = d
            if s == own:
                continue
            if v and (s not in best or best[s] < v):
                best[s] = v
        for s, v in best.items():
            self.wait_raw(eng, s, v)

    def emit(self, eng, meth, *a, sem=None, inc=1, **kw):
        tgt = self.inc(sem, inc) if sem else None

        def fn(E, meth=meth, a=a, kw=kw, sem=sem, inc=inc):
            ins = getattr(E, meth)(*a, **kw)
            if sem:
                ins.then_inc(self.S[sem], inc)

        self.op(eng, fn)
        return (sem, tgt) if sem else None

    def do(self, eng, meth, *a, reads=(), writes=(), sem=None, inc=1,
           extra=(), **kw):
        deps = []
        for r in reads:
            deps.append(r.wdep)
        for w in writes:
            deps.append(w.wdep)
            deps.extend(w.rdeps)
        deps.extend(extra)
        self.waits(eng, deps)
        v = self.emit(eng, meth, *a, sem=sem, inc=inc, **kw)
        if v is not None:
            for r in reads:
                r.rdeps.append(v)
            for w in writes:
                w.wdep = v
                w.rdeps = []
        return v

    def dma(self, eng, out, in_, sem, reads=(), writes=(), extra=()):
        return self.do(eng, "dma_start", reads=reads, writes=writes, sem=sem,
                       inc=16, extra=extra, out=out, in_=in_)

    def dma_batch(self, eng, sem, items, writes=(), reads=(), extra=()):
        deps = list(extra)
        for r in reads:
            deps.append(r.wdep)
        for w in writes:
            deps.append(w.wdep)
            deps.extend(w.rdeps)
        self.waits(eng, deps)
        v = None
        for out, in_ in items:
            v = self.emit(eng, "dma_start", sem=sem, inc=16, out=out, in_=in_)
        for r in reads:
            r.rdeps.append(v)
        for w in writes:
            w.wdep = v
            w.rdeps = []
        return v

    def fence(self):
        vals = {s: self.cnt.get(s, 0) for s in self.S}
        for eng in ENGS:
            own = self.OWN.get(eng)
            for s, v in vals.items():
                if s != own:
                    self.wait_raw(eng, s, v)


def _build_program():
    nc = bass.Bass("TRN2", target_bir_lowering=False)
    k = KB(nc)

    x_in = nc.dram_tensor("x_in", [T, D], F32, kind="ExternalInput").ap()
    xT_in = nc.dram_tensor("xT_in", [D, T], F32, kind="ExternalInput").ap()
    y_out = nc.dram_tensor("y_out", [T, D], F32, kind="ExternalOutput").ap()
    dbg = Res(nc.dram_tensor("dbg", [128, 8192], F32, kind="ExternalOutput").ap())

    W = {}
    for nm, shp in [
        ("l0_wq", [D, D]), ("l0_wk", [D, D]), ("l0_wv", [D, D]), ("l0_wo", [D, D]),
        ("l0_wg", [I0, D]), ("l0_wu", [I0, D]), ("l0_wd", [D, I0]),
        ("l1_wq", [D, D]), ("l1_wk", [D, D]), ("l1_wv", [D, D]), ("l1_wo", [D, D]),
        ("l1_wg1", [D, HD]), ("l1_wg2", [HD, D]),
        ("l1_w1", [I1, D]), ("l1_w3", [I1, D]), ("l1_w2", [D, I1]),
    ]:
        W[nm] = nc.dram_tensor(nm, shp, F32, kind="ExternalInput").ap()

    C = {}
    for nm, shp, dt in [
        ("ident_b", [128, 128], BF16), ("ident_f", [128, 128], F32),
        ("ones_f", [128, 512], F32),
        ("bq_sb", [128, NH], F32), ("bk_sb", [128, NH], F32),
        ("bv_sb", [128, NH], F32),
        ("bo_rep", [128, D], F32), ("nw_rep", [128, D], F32),
        ("cs_t", [128, T], F32), ("sc_t", [128, T], F32),
        ("mrot", [128, 128], F32),
        ("dmask", [128, NH * 128], F32),
        ("deck", [128, NH], F32), ("decq", [128, NH], F32),
        ("flag", [128, 1], F32), ("flagi", [128, 1], F32),
    ]:
        C[nm] = nc.dram_tensor(nm, shp, dt, kind="ExternalInput").ap()

    spill_x = nc.dram_tensor("spill_x", [T, D], F32).ap()
    xt_il = nc.dram_tensor("xt_il", [2 * D, T], BF16).ap()
    wgT_q = nc.dram_tensor("wgT_q", [D, I0P], BF16).ap()
    wuT_q = nc.dram_tensor("wuT_q", [D, I0P], BF16).ap()
    wdT_q = nc.dram_tensor("wdT_q", [I0P, D], BF16).ap()
    w1T_q = nc.dram_tensor("w1T_q", [D, I1P], BF16).ap()
    w3T_q = nc.dram_tensor("w3T_q", [D, I1P], BF16).ap()
    w2T_q = nc.dram_tensor("w2T_q", [I1P, D], BF16).ap()
    xt_h = nc.dram_tensor("xt_h", [D, T], BF16).ap()
    xt_l = nc.dram_tensor("xt_l", [D, T], BF16).ap()
    cc0_in = [nc.dram_tensor(f"cc0i{g}", [128, 1], F32).ap() for g in range(NH)]
    cc0_out = [nc.dram_tensor(f"cc0o{g}", [128, 1], F32).ap() for g in range(NH)]
    cc1_in = [nc.dram_tensor(f"cc1i{g}", [128, 128], F32).ap()
              for g in range(NH)]
    cc1_out = [nc.dram_tensor(f"cc1o{g}", [128, 128], F32).ap()
               for g in range(NH)]
    PAIRS = [[0, 1], [2, 3], [4, 5], [6, 7]]

    with contextlib.ExitStack() as ctx:
        for s in ["sx", "sw", "sh", "sh2", "so", "sc", "st", "sq",
                  "mm", "vv", "aa", "gg", "cc"]:
            k.S[s] = ctx.enter_context(nc.semaphore(s))

        def sbuf(name, shape, dt=F32):
            return Res(ctx.enter_context(nc.sbuf_tensor(name, shape, dt)))

        x_tm = [sbuf(f"x_tm{i}", [128, D]) for i in range(NTC)]
        consts = {}
        for nm, shape, dt in [("ident_b", [128, 128], BF16),
                              ("ident_f", [128, 128], F32),
                              ("ones_f", [128, 512], F32),
                              ("bq_sb", [128, NH], F32),
                              ("bk_sb", [128, NH], F32),
                              ("bv_sb", [128, NH], F32),
                              ("flag", [128, 1], F32),
                              ("flagi", [128, 1], F32)]:
            consts[nm] = sbuf("c_" + nm, shape, dt)
        pb = [Res(ctx.enter_context(nc.psum_tensor(f"pb{i}", [128, 512], F32)))
              for i in range(8)]

        k.dma_batch("sync", "sx",
                    [(x_tm[i].ap[:, :], x_in[i * 128:(i + 1) * 128, :])
                     for i in range(NTC)]
                    + [(consts[nm].ap[:, :], C[nm]) for nm in consts],
                    writes=x_tm + list(consts.values()))

        with nc.sbuf_tensor("sp_f", [128, T], F32) as sp_f_t, \
             nc.sbuf_tensor("sp_h", [128, T], BF16) as sp_h_t, \
             nc.sbuf_tensor("sp_l", [128, T], BF16) as sp_l_t:
            sp_f, sp_h, sp_l = Res(sp_f_t), Res(sp_h_t), Res(sp_l_t)
            for dc in range(NDC):
                k.dma("sync", sp_f.ap[:, :], xT_in[dc * 128:(dc + 1) * 128, :],
                      "sx", writes=[sp_f])
                k.do("scalar", "activation", out=sp_h.ap[:, :],
                     in_=sp_f.ap[:, :], func=AF.Identity, reads=[sp_f],
                     writes=[sp_h], sem="aa")
                k.do("vector", "tensor_sub", sp_l.ap[:, :], sp_f.ap[:, :],
                     sp_h.ap[:, :], reads=[sp_f, sp_h], writes=[sp_l], sem="vv")
                k.dma_batch("gpsimd", "so",
                            [(xt_il[dc * 256:dc * 256 + 128, :],
                              sp_h.ap[:, :]),
                             (xt_il[dc * 256 + 128:dc * 256 + 256, :],
                              sp_l.ap[:, :])],
                            reads=[sp_h, sp_l])
        v_xt_spill = ("so", k.val("so"))
        k.fence()

        cfg = dict(nc=nc, k=k, W=W, C=C, x_tm=x_tm, pb=pb, ctx=ctx,
                   consts=consts, xt_h=xt_h, xt_l=xt_l, spill_x=spill_x,
                   cc0_in=cc0_in, cc0_out=cc0_out, cc1_in=cc1_in,
                   cc1_out=cc1_out, PAIRS=PAIRS, v_xt_spill=v_xt_spill,
                   xt_il=xt_il,
                   wgT_q=wgT_q, wuT_q=wuT_q, wdT_q=wdT_q, x_in=x_in,
                   w1T_q=w1T_q, w3T_q=w3T_q, w2T_q=w2T_q, dbg=dbg)

        if _N_STAGES >= 1:
            _stage_l0_attn(cfg)
        k.fence()
        if _N_STAGES >= 2:
            _stage_l0_ffn(cfg)
            k.fence()
        if _N_STAGES >= 3:
            _stage_l1_attn(cfg)
            k.fence()
        if _N_STAGES >= 4:
            _stage_l1_ffn(cfg)
            k.fence()

        vst = k.dma_batch("gpsimd", "st",
                          [(y_out[i * 128:(i + 1) * 128, :], x_tm[i].ap[:, :])
                           for i in range(NTC)], reads=x_tm,
                          extra=[v_xt_spill])
        k.waits("gpsimd", [vst])

        with nc.Block() as block:
            for en in ENGS:
                def mk(en):
                    def f(E):
                        for fn in k.ops[en]:
                            fn(E)
                    return f
                getattr(block, en)(mk(en))

    return nc


def _split(k, src, hi, lo, scale=None, cols=None, src_cols=None):
    sl = slice(None) if cols is None else cols
    ssl = sl if src_cols is None else src_cols
    if scale is None:
        k.do("scalar", "activation", out=hi.ap[:, sl], in_=src.ap[:, ssl],
             func=AF.Identity, reads=[src], writes=[hi], sem="aa")
        k.do("vector", "tensor_sub", lo.ap[:, sl], src.ap[:, ssl],
             hi.ap[:, sl], reads=[src, hi], writes=[lo], sem="vv")
    else:
        k.do("scalar", "activation", out=hi.ap[:, sl], in_=src.ap[:, ssl],
             func=AF.Identity, scale=float(scale), reads=[src], writes=[hi],
             sem="aa")
        k.do("vector", "scalar_tensor_tensor", lo.ap[:, sl], src.ap[:, ssl],
             float(scale), hi.ap[:, sl], ALU.mult, ALU.subtract,
             reads=[src, hi], writes=[lo], sem="vv")


def _transpose16(k, pb, ident_b, src, dst, bankno=4, n=16, src_off=0,
                 dst_off=0):
    for grp in range((n + 3) // 4):
        bank = pb[bankno + grp % 2]
        lo = grp * 4
        hi = min(lo + 4, n)
        for q in range(hi - lo):
            c0 = src_off + (lo + q) * 128
            k.do("tensor", "matmul", bank.ap[:, q * 128:(q + 1) * 128],
                 src.ap[:, c0:c0 + 128], ident_b.ap[:, :],
                 start=True, stop=True, reads=[src, ident_b], writes=[bank],
                 sem="mm" if q == hi - lo - 1 else None)
        w = (hi - lo) * 128
        d0 = dst_off + lo * 128
        k.do("scalar", "activation", out=dst.ap[:, d0:d0 + w],
             in_=bank.ap[:, 0:w], func=AF.Identity, reads=[bank],
             writes=[dst], sem="aa")


def _stage_l0_attn(cfg):
    k = cfg["k"]; nc = cfg["nc"]; W = cfg["W"]; C = cfg["C"]
    x_tm = cfg["x_tm"]; pb = cfg["pb"]; co = cfg["consts"]
    xt_h = cfg["xt_h"]; xt_l = cfg["xt_l"]
    ident_b = co["ident_b"]; ones_f = co["ones_f"]; flag = co["flag"]
    flagi = co["flagi"]
    xt_il = cfg["xt_il"]
    bq_sb = co["bq_sb"]; bk_sb = co["bk_sb"]; bv_sb = co["bv_sb"]
    cc0_in = cfg["cc0_in"]; cc0_out = cfg["cc0_out"]; PAIRS = cfg["PAIRS"]

    with contextlib.ExitStack() as st:
        def sbuf(name, shape, dt=F32):
            return Res(st.enter_context(nc.sbuf_tensor(name, shape, dt)))

        w_f = sbuf("a0_wf", [128, D])
        w_h = sbuf("a0_wh", [128, 1024], BF16)
        w_l = sbuf("a0_wl", [128, 1024], BF16)
        wT = [[sbuf(f"a0_wT{i}{hl}", [128, D], BF16) for hl in range(2)]
              for i in range(2)]
        xtb2 = [sbuf(f"a0_xtb{i}", [128, 8 * 512], BF16) for i in range(2)]
        t1 = sbuf("a0_t1", [128, 512])
        t2 = sbuf("a0_t2", [128, 512])
        t3 = sbuf("a0_t3", [128, 8])
        kv = sbuf("a0_kv", [128, T])
        kvcar = sbuf("a0_kvcar", [128, 1])
        zsum = sbuf("a0_zsum", [128, 1])
        car0 = sbuf("a0_car0", [128, 1])
        ctx_h = [sbuf(f"a0_cxh{i}", [128, T], BF16) for i in range(2)]
        ctx_l = [sbuf(f"a0_cxl{i}", [128, T], BF16) for i in range(2)]
        woT_h = sbuf("a0_woTh", [128, 2 * 512], BF16)
        woT_l = sbuf("a0_woTl", [128, 2 * 512], BF16)
        sq = sbuf("a0_sq", [128, 1])
        msc = sbuf("a0_msc", [128, 2 * NTC])
        k.fence()

        def load_wT(wap, j, dst):
            k.dma("sync", w_f.ap[:, :], wap[j * 128:(j + 1) * 128, :], "sw",
                  writes=[w_f])
            for hf in range(2):
                sc = slice(hf * 1024, (hf + 1) * 1024)
                _split(k, w_f, w_h, w_l, cols=slice(0, 1024), src_cols=sc)
                _transpose16(k, pb, ident_b, w_h, dst[0], n=8,
                             dst_off=hf * 1024)
                _transpose16(k, pb, ident_b, w_l, dst[1], n=8,
                             dst_off=hf * 1024)

        def load_xtb(tau, quarter, par):
            xtb = xtb2[par]
            c0 = tau * 512
            r0 = quarter * 1024
            k.dma_batch("sync" if par == 0 else "scalar",
                        "sh" if par == 0 else "sh2",
                        [(xtb.ap[:, :].rearrange("p (b c) -> p b c", c=512),
                          xt_il[r0:r0 + 1024, c0:c0 + 512].rearrange(
                              "(b p) c -> p b c", p=128))],
                        writes=[xtb], extra=[cfg["v_xt_spill"]])

        def proj_half(wpair, quarter, bankno, first, par):
            xtb = xtb2[par]
            bank = pb[bankno]
            wh, wl = wpair
            for i in range(4):
                dc = quarter * 4 + i
                lh = wh.ap[:, dc * 128:(dc + 1) * 128]
                ll = wl.ap[:, dc * 128:(dc + 1) * 128]
                rh = xtb.ap[:, (2 * i) * 512:(2 * i + 1) * 512]
                rl = xtb.ap[:, (2 * i + 1) * 512:(2 * i + 2) * 512]
                st0 = first and i == 0
                last = (quarter == 3 and i == 3)
                k.do("tensor", "matmul", bank.ap[:, :], lh, rh,
                     start=st0, stop=False, reads=[wh, xtb], writes=[bank])
                k.do("tensor", "matmul", bank.ap[:, :], lh, rl,
                     start=False, stop=False, reads=[wh, xtb], writes=[bank])
                k.do("tensor", "matmul", bank.ap[:, :], ll, rh,
                     start=False, stop=last,
                     reads=[wh, wl, xtb], writes=[bank],
                     sem="mm" if i == 3 else None)

        def phi_chain(bank, bias, j, out):
            k.do("vector", "tensor_scalar", out.ap[:, :], bank.ap[:, :],
                 bias.ap[:, j:j + 1], 0.0, ALU.add, ALU.min,
                 reads=[bank, bias], writes=[out], sem="vv")
            k.do("vector", "tensor_scalar", t2.ap[:, :], bank.ap[:, :],
                 bias.ap[:, j:j + 1], 0.0, ALU.add, ALU.max,
                 reads=[bank, bias], writes=[t2], sem="vv")
            k.do("scalar", "activation", out=out.ap[:, :], in_=out.ap[:, :],
                 func=AF.Exp, reads=[out], writes=[out], sem="aa")
            k.do("vector", "tensor_add", out.ap[:, :], out.ap[:, :],
                 t2.ap[:, :], reads=[t2], writes=[out], sem="vv")

        k.dma("sync", w_f.ap[:, :], C["bo_rep"], "sw", writes=[w_f])
        for tc in range(NTC):
            k.do("vector", "tensor_add", x_tm[tc].ap[:, :], x_tm[tc].ap[:, :],
                 w_f.ap[:, :], reads=[w_f], writes=[x_tm[tc]], sem="vv")

        for j in range(NH):
            load_wT(W["l0_wk"], j, wT[0])
            load_wT(W["l0_wv"], j, wT[1])
            k.do("vector", "memset", kvcar.ap[:, :], 0.0, writes=[kvcar],
                 sem="vv")
            k.do("vector", "memset", zsum.ap[:, :], 0.0, writes=[zsum],
                 sem="vv")
            for tau in range(4):
                bk0, bk1 = (tau % 2) * 2, (tau % 2) * 2 + 1
                for qtr in range(4):
                    par = (tau * 4 + qtr) % 2
                    load_xtb(tau, qtr, par)
                    proj_half(wT[0], qtr, bk0, first=(qtr == 0), par=par)
                    proj_half(wT[1], qtr, bk1, first=(qtr == 0), par=par)
                cols = slice(tau * 512, (tau + 1) * 512)
                phi_chain(pb[bk0], bk_sb, j, t1)
                k.do("vector", "tensor_scalar", t2.ap[:, :], pb[bk1].ap[:, :],
                     bv_sb.ap[:, j:j + 1], None, ALU.add,
                     reads=[pb[bk1], bv_sb], writes=[t2], sem="vv")
                k.do("vector", "tensor_mul", t2.ap[:, :], t1.ap[:, :],
                     t2.ap[:, :], reads=[t1], writes=[t2], sem="vv")
                k.do("vector", "reduce_sum", t3.ap[:, 0:1], t2.ap[:, :],
                     axis=AXX, reads=[t2], writes=[t3], sem="vv")
                k.do("vector", "tensor_add", zsum.ap[:, :], zsum.ap[:, :],
                     t3.ap[:, 0:1], reads=[t3], writes=[zsum], sem="vv")
                k.do("vector", "tensor_tensor_scan", kv.ap[:, cols],
                     ones_f.ap[:, :], t2.ap[:, :], kvcar.ap[:, :],
                     ALU.mult, ALU.add, reads=[ones_f, t2, kvcar],
                     writes=[kv], sem="vv")
                k.do("vector", "tensor_copy", kvcar.ap[:, :],
                     kv.ap[:, tau * 512 + 511:tau * 512 + 512],
                     reads=[kv], writes=[kvcar], sem="vv")

            k.do("gpsimd", "tensor_scalar", t3.ap[:, 1:2], zsum.ap[:, :],
                 flagi.ap[:, 0:1], None, ALU.mult, reads=[zsum, flagi],
                 writes=[t3], sem="gg")
            vz = k.dma("gpsimd", cc0_in[j][:, :], t3.ap[:, 1:2], "sc",
                       reads=[t3])
            k.do("gpsimd", "collective_compute", "AllReduce", ALU.add,
                 PAIRS, [cc0_in[j][:, :]], [cc0_out[j][:, :]], sem="cc",
                 extra=[vz])
            k.dma("gpsimd", car0.ap[:, :], cc0_out[j][:, :], "sc",
                  writes=[car0], extra=[("cc", k.val("cc"))])
            k.do("gpsimd", "tensor_scalar", car0.ap[:, :], car0.ap[:, :],
                 flag.ap[:, 0:1], None, ALU.mult, reads=[flag], writes=[car0],
                 sem="gg")

            load_wT(W["l0_wq"], j, wT[0])
            slot = j % 2
            for tau in range(4):
                bk0 = (tau % 2) * 2
                for qtr in range(4):
                    par = (tau * 4 + qtr) % 2
                    load_xtb(tau, qtr, par)
                    proj_half(wT[0], qtr, bk0, first=(qtr == 0), par=par)
                cols = slice(tau * 512, (tau + 1) * 512)
                phi_chain(pb[bk0], bq_sb, j, t1)
                k.do("vector", "tensor_scalar", t2.ap[:, :], kv.ap[:, cols],
                     car0.ap[:, 0:1], None, ALU.add, reads=[kv, car0],
                     writes=[t2], sem="vv")
                k.do("vector", "tensor_mul", t2.ap[:, :], t1.ap[:, :],
                     t2.ap[:, :], reads=[t1], writes=[t2], sem="vv")
                _split(k, t2, ctx_h[slot], ctx_l[slot], scale=SCALE0,
                       cols=cols, src_cols=slice(0, 512))

            if j % 2 == 1:
                for et in range(4):
                    for s2 in range(2):
                        jo = j - 1 + s2
                        k.dma_batch(
                            "sync", "sw",
                            [(w_f.ap[:, q * 128:(q + 1) * 128],
                              W["l0_wo"][et * 512 + q * 128:
                                         et * 512 + (q + 1) * 128,
                                         jo * 128:(jo + 1) * 128])
                             for q in range(4)], writes=[w_f])
                        _split(k, w_f, w_h, w_l, cols=slice(0, 512))
                        _transpose16(k, pb, ident_b, w_h, woT_h, n=4,
                                     dst_off=s2 * 512)
                        _transpose16(k, pb, ident_b, w_l, woT_l, n=4,
                                     dst_off=s2 * 512)
                    for tc in range(NTC):
                        bank = pb[6 + tc % 2]
                        first = True
                        for s2 in range(2):
                            ch = ctx_h[s2]; cl = ctx_l[s2]
                            csl = slice(tc * 128, (tc + 1) * 128)
                            wsl = slice(s2 * 512, (s2 + 1) * 512)
                            k.do("tensor", "matmul", bank.ap[:, :],
                                 ch.ap[:, csl], woT_h.ap[:, wsl],
                                 start=first, stop=False,
                                 reads=[ch, woT_h], writes=[bank])
                            first = False
                            k.do("tensor", "matmul", bank.ap[:, :],
                                 ch.ap[:, csl], woT_l.ap[:, wsl],
                                 start=False, stop=False,
                                 reads=[ch, woT_l], writes=[bank])
                            k.do("tensor", "matmul", bank.ap[:, :],
                                 cl.ap[:, csl], woT_h.ap[:, wsl],
                                 start=False, stop=(s2 == 1),
                                 reads=[ch, cl, woT_h, woT_l], writes=[bank],
                                 sem="mm" if s2 == 1 else None)
                        ecols = slice(et * 512, (et + 1) * 512)
                        k.do("vector", "tensor_add", x_tm[tc].ap[:, ecols],
                             x_tm[tc].ap[:, ecols], bank.ap[:, :],
                             reads=[bank], writes=[x_tm[tc]], sem="vv")

        _layernorm(k, x_tm, msc, w_f, sq)


def _layernorm(k, x_tm, msc, lnscr, sq, chunks=None):
    n = len(x_tm) if chunks is None else len(chunks)
    tcs = range(len(x_tm)) if chunks is None else chunks
    for ii, tc in enumerate(tcs):
        k.do("vector", "reduce_sum", msc.ap[:, ii:ii + 1], x_tm[tc].ap[:, :],
             axis=AXX, reads=[x_tm[tc]], writes=[msc], sem="vv")
        k.do("scalar", "activation", out=lnscr.ap[:, :],
             in_=x_tm[tc].ap[:, :], func=AF.Square,
             accum_out=sq.ap[:, 0:1], reads=[x_tm[tc]], writes=[lnscr, sq],
             sem="aa")
        k.do("vector", "tensor_copy", msc.ap[:, n + ii:n + ii + 1],
             sq.ap[:, 0:1], reads=[sq], writes=[msc], sem="vv")
    k.do("vector", "tensor_scalar", msc.ap[:, 0:n], msc.ap[:, 0:n],
         1.0 / D, None, ALU.mult, reads=[msc], writes=[msc], sem="vv")
    k.do("vector", "tensor_scalar", msc.ap[:, n:2 * n], msc.ap[:, n:2 * n],
         1.0 / D, None, ALU.mult, reads=[msc], writes=[msc], sem="vv")
    k.do("vector", "tensor_mul", lnscr.ap[:, 0:n], msc.ap[:, 0:n],
         msc.ap[:, 0:n], reads=[msc], writes=[lnscr], sem="vv")
    k.do("vector", "tensor_sub", msc.ap[:, n:2 * n], msc.ap[:, n:2 * n],
         lnscr.ap[:, 0:n], reads=[lnscr], writes=[msc], sem="vv")
    k.do("vector", "tensor_scalar", msc.ap[:, n:2 * n], msc.ap[:, n:2 * n],
         1e-5, None, ALU.add, reads=[msc], writes=[msc], sem="vv")
    k.do("scalar", "activation", out=msc.ap[:, n:2 * n],
         in_=msc.ap[:, n:2 * n], func=AF.Sqrt, reads=[msc],
         writes=[msc], sem="aa")
    k.do("vector", "reciprocal", msc.ap[:, n:2 * n], msc.ap[:, n:2 * n],
         reads=[msc], writes=[msc], sem="vv")
    for ii, tc in enumerate(tcs):
        k.do("vector", "tensor_scalar", x_tm[tc].ap[:, :], x_tm[tc].ap[:, :],
             msc.ap[:, ii:ii + 1], msc.ap[:, n + ii:n + ii + 1],
             ALU.subtract, ALU.mult, reads=[msc], writes=[x_tm[tc]], sem="vv")


def _stage_l0_ffn(cfg):
    k = cfg["k"]; nc = cfg["nc"]; W = cfg["W"]
    x_tm = cfg["x_tm"]; pb = cfg["pb"]; co = cfg["consts"]
    ident_b = co["ident_b"]; ones_f = co["ones_f"]
    spill_x = cfg["spill_x"]
    wgT_q = cfg["wgT_q"]; wuT_q = cfg["wuT_q"]; wdT_q = cfg["wdT_q"]
    dbg = cfg["dbg"]
    EPS = 1e-5
    HT = [(i * 512, 512) for i in range(12)] + [(6144, 384)]

    with contextlib.ExitStack() as st:
        def sbuf(name, shape, dt=F32):
            return Res(st.enter_context(nc.sbuf_tensor(name, shape, dt)))

        w_f = sbuf("f0_wf", [128, D])
        wq_b = sbuf("f0_wqb", [128, D], BF16)
        tx_b = sbuf("f0_txb", [128, D], BF16)
        t_a = sbuf("f0_ta", [128, D])
        t_b = sbuf("f0_tb", [128, D])
        i_t = sbuf("f0_it", [128, D], mybir.dt.int32)
        scq = sbuf("f0_scq", [128, 8])
        rx = sbuf("f0_rx", [128, 12])
        mx2 = sbuf("f0_mx2", [128, 2])
        acc = sbuf("f0_acc", [128, 2])
        sm = sbuf("f0_sm", [1, 8])
        msc = sbuf("f0_msc", [128, 8])
        sq = sbuf("f0_sq", [128, 1])
        dbg_sb = sbuf("f0_dbgsb", [128, 512])
        k.fence()

        def dbg_cp(c0, src_ap, src_res, w, psum=False):
            if not _DBG:
                return
            if psum:
                k.do("vector", "tensor_copy", dbg_sb.ap[:, 0:w], src_ap,
                     reads=[src_res], writes=[dbg_sb], sem="vv")
                k.dma("gpsimd", dbg.ap[:, c0:c0 + w], dbg_sb.ap[:, 0:w],
                      "st", reads=[dbg_sb])
            else:
                k.do("vector", "tensor_copy", dbg_sb.ap[:, 0:w], src_ap,
                     reads=[src_res], writes=[dbg_sb], sem="vv")
                k.dma("gpsimd", dbg.ap[:, c0:c0 + w], dbg_sb.ap[:, 0:w],
                      "st", reads=[dbg_sb])

        k.dma_batch("gpsimd", "so",
                    [(spill_x[i * 128:(i + 1) * 128, :], x_tm[i].ap[:, :])
                     for i in range(NTC)], reads=x_tm)
        v_spill = ("so", k.val("so"))

        def wchunks(nm):
            if nm in ("l0_wg", "l0_wu"):
                for r0 in range(0, I0, 128):
                    nr = min(128, I0 - r0)
                    yield W[nm][r0:r0 + nr, :], nr, D
            else:
                for r0 in range(0, D, 128):
                    for c0 in range(0, I0, 2048):
                        w = min(2048, I0 - c0)
                        yield W[nm][r0:r0 + 128, c0:c0 + w], 128, w

        for mi, nm in enumerate(["l0_wg", "l0_wu", "l0_wd"]):
            k.do("vector", "memset", acc.ap[:, 0:1], 0.0, writes=[acc],
                 sem="vv")
            for sl, nr, wcols in wchunks(nm):
                k.dma("sync", w_f.ap[0:nr, 0:wcols], sl, "sw", writes=[w_f])
                k.do("scalar", "activation", out=t_a.ap[0:nr, 0:wcols],
                     in_=w_f.ap[0:nr, 0:wcols], func=AF.Abs, reads=[w_f],
                     writes=[t_a], sem="aa")
                k.do("vector", "tensor_reduce", acc.ap[0:nr, 1:2],
                     t_a.ap[0:nr, 0:wcols], axis=AXX, op=ALU.add,
                     reads=[t_a], writes=[acc], sem="vv")
                k.do("vector", "tensor_add", acc.ap[0:nr, 0:1],
                     acc.ap[0:nr, 0:1], acc.ap[0:nr, 1:2], reads=[acc],
                     writes=[acc], sem="vv")
            k.do("tensor", "matmul", pb[5].ap[0:1, 0:1], acc.ap[:, 0:1],
                 ones_f.ap[:, 0:1], start=True, stop=True,
                 reads=[acc, ones_f], writes=[pb[5]], sem="mm")
            k.do("vector", "tensor_scalar", sm.ap[0:1, 0:1],
                 pb[5].ap[0:1, 0:1], 1.0 / (I0 * D), EPS, ALU.mult, ALU.max,
                 reads=[pb[5]], writes=[sm], sem="vv")
            k.do("vector", "reciprocal", sm.ap[0:1, 1:2], sm.ap[0:1, 0:1],
                 reads=[sm], writes=[sm], sem="vv")
            k.do("tensor", "matmul", pb[5].ap[:, 0:2], ones_f.ap[0:1, 0:128],
                 sm.ap[0:1, 0:2], start=True, stop=True, reads=[ones_f, sm],
                 writes=[pb[5]], sem="mm")
            k.do("vector", "tensor_copy", scq.ap[:, 2 * mi:2 * mi + 1],
                 pb[5].ap[:, 1:2], reads=[pb[5]], writes=[scq], sem="vv")
            k.do("vector", "tensor_copy", scq.ap[:, 2 * mi + 1:2 * mi + 2],
                 pb[5].ap[:, 0:1], reads=[pb[5]], writes=[scq], sem="vv")

        def quant_chain(eng, src, dst_ap, nr, wcols, sc_col, lo, hi,
                        dst_res, src_c0=0):
            sem = {"vector": "vv", "gpsimd": "gg"}[eng]
            k.do(eng, "tensor_scalar", t_a.ap[0:nr, 0:wcols],
                 src.ap[0:nr, src_c0:src_c0 + wcols], sc_col[0:nr, :], 0.5,
                 ALU.mult, ALU.add, reads=[src, scq, rx], writes=[t_a],
                 sem=sem)
            k.do(eng, "tensor_scalar_add", i_t.ap[0:nr, 0:wcols],
                 t_a.ap[0:nr, 0:wcols], 0.0, reads=[t_a], writes=[i_t],
                 sem=sem)
            k.do(eng, "tensor_tensor", t_b.ap[0:nr, 0:wcols],
                 i_t.ap[0:nr, 0:wcols], t_a.ap[0:nr, 0:wcols], ALU.is_gt,
                 reads=[i_t, t_a], writes=[t_b], sem=sem)
            k.do(eng, "tensor_sub", t_a.ap[0:nr, 0:wcols],
                 i_t.ap[0:nr, 0:wcols], t_b.ap[0:nr, 0:wcols],
                 reads=[i_t, t_b], writes=[t_a], sem=sem)
            k.do(eng, "tensor_scalar", dst_ap, t_a.ap[0:nr, 0:wcols],
                 float(hi), float(lo), ALU.min, ALU.max, reads=[t_a],
                 writes=[dst_res], sem=sem)

        def transpose_store(ntile, dram, r0c0):
            _transpose16(k, pb, ident_b, wq_b, tx_b, n=ntile)
            h0, c0 = r0c0
            out = dram[h0:h0 + ntile * 128, c0:c0 + 128].rearrange(
                "(dc p) c -> p dc c", p=128)
            in_ = tx_b.ap[:, 0:ntile * 128].rearrange(
                "p (dc c) -> p dc c", c=128)
            k.dma("gpsimd", out, in_, "sq", reads=[tx_b])

        qeng = ["vector", "vector"]
        qi = 0
        for nm, dram, sc_i in [("l0_wg", wgT_q, 0), ("l0_wu", wuT_q, 2)]:
            for ci, r0 in enumerate(range(0, I0P, 128)):
                nr = min(128, I0 - r0) if r0 < I0 else 0
                if nr < 128:
                    k.do("vector", "memset", wq_b.ap[nr:128, :], 0.0,
                         writes=[wq_b], sem="vv")
                if nr > 0:
                    k.dma("sync", w_f.ap[0:nr, :], W[nm][r0:r0 + nr, :], "sw",
                          writes=[w_f])
                    quant_chain(qeng[qi % 2], w_f, wq_b.ap[0:nr, :], nr, D,
                                scq.ap[:, sc_i:sc_i + 1], -1.0, 1.0, wq_b)
                    qi += 1
                _transpose16(k, pb, ident_b, wq_b, tx_b, n=16)
                out = dram[0:D, ci * 128:(ci + 1) * 128].rearrange(
                    "(dc p) c -> p dc c", p=128)
                in_ = tx_b.ap[:, :].rearrange("p (dc c) -> p dc c", c=128)
                k.dma("gpsimd", out, in_, "sq", reads=[tx_b])
        for ci, r0 in enumerate(range(0, D, 128)):
            for pi, c0 in enumerate(range(0, I0P, 2048)):
                wreal = max(0, min(2048, I0 - c0))
                wpad = min(2048, I0P - c0)
                ntile = wpad // 128
                if wreal < wpad:
                    k.do("vector", "memset", wq_b.ap[:, wreal:wpad], 0.0,
                         writes=[wq_b], sem="vv")
                k.dma("sync", w_f.ap[:, 0:wreal], W["l0_wd"][r0:r0 + 128,
                                                             c0:c0 + wreal],
                      "sw", writes=[w_f])
                quant_chain(qeng[qi % 2], w_f, wq_b.ap[:, 0:wreal], 128,
                            wreal, scq.ap[:, 4:5], -1.0, 1.0, wq_b)
                qi += 1
                transpose_store(ntile, wdT_q, (c0, ci * 128))

        dbg_cp(0, scq.ap[:, 0:8], scq, 8)
        v_qcache = ("sq", k.val("sq"))

        for g in range(8):
            G = [2 * g, 2 * g + 1]
            scr = [i for i in range(NTC) if i not in G]
            gu_t = [scr[0:4], scr[4:8]]
            wg_t = [G[0], scr[8]]
            wu_t = [G[1], scr[9]]
            nT_t = scr[10]
            nbf_t = scr[11]
            n2bf_t = [scr[12], scr[13]]
            xg_t = [scr[12], scr[13]]
            xr_t = [scr[10], scr[11]]
            n2T_t = gu_t[0]
            dn_t = gu_t[1] + [wg_t[0], wg_t[1], wu_t[0]]

            def bfv(ti):
                return x_tm[ti].ap.bitcast(BF16)

            k.dma_batch("sync", "sx",
                        [(x_tm[xg_t[tc]].ap[:, :],
                          spill_x[G[tc] * 128:(G[tc] + 1) * 128, :])
                         for tc in range(2)],
                        writes=[x_tm[xg_t[0]], x_tm[xg_t[1]]],
                        extra=[v_spill])
            for tc in range(2):
                xres = x_tm[xg_t[tc]]
                k.do("vector", "reduce_max", rx.ap[:, 8 + tc:9 + tc],
                     xres.ap[:, :], axis=AXX, apply_absolute_value=True,
                     reads=[xres], writes=[rx], sem="vv")
                k.do("vector", "tensor_scalar", rx.ap[:, 8 + tc:9 + tc],
                     rx.ap[:, 8 + tc:9 + tc], EPS, 1.0 / 127.0, ALU.max,
                     ALU.mult, reads=[rx], writes=[rx], sem="vv")
                k.do("vector", "tensor_mul", rx.ap[:, 0 + tc:1 + tc],
                     rx.ap[:, 8 + tc:9 + tc], scq.ap[:, 1:2], reads=[rx, scq],
                     writes=[rx], sem="vv")
                k.do("vector", "tensor_mul", rx.ap[:, 2 + tc:3 + tc],
                     rx.ap[:, 8 + tc:9 + tc], scq.ap[:, 3:4], reads=[rx, scq],
                     writes=[rx], sem="vv")
                k.do("vector", "reciprocal", rx.ap[:, 10 + tc:11 + tc],
                     rx.ap[:, 8 + tc:9 + tc], reads=[rx], writes=[rx],
                     sem="vv")
                n_ap = bfv(nbf_t)[:, tc * D:(tc + 1) * D]
                quant_chain("vector", xres, n_ap, 128, D,
                            rx.ap[:, 10 + tc:11 + tc], -128.0, 127.0,
                            x_tm[nbf_t])
                if g == 0 and tc == 0:
                    dbg_cp(8, rx.ap[:, 0:12], rx, 12)
                    dbg_cp(512, bfv(nbf_t)[:, 0:512], x_tm[nbf_t], 512)
                for dcg in range(4):
                    bank = pb[4]
                    for q in range(4):
                        dc = dcg * 4 + q
                        k.do("tensor", "matmul",
                             bank.ap[:, q * 128:(q + 1) * 128],
                             bfv(nbf_t)[:, tc * D + dc * 128:
                                        tc * D + (dc + 1) * 128],
                             ident_b.ap[:, :], start=True, stop=True,
                             reads=[x_tm[nbf_t], ident_b], writes=[bank],
                             sem="mm" if q == 3 else None)
                    for q in range(4):
                        dc = dcg * 4 + q
                        k.do("scalar", "activation",
                             out=bfv(nT_t)[:, dc * 256 + tc * 128:
                                           dc * 256 + (tc + 1) * 128],
                             in_=bank.ap[:, q * 128:(q + 1) * 128],
                             func=AF.Identity, reads=[bank],
                             writes=[x_tm[nT_t]], sem="aa")

            k.do("vector", "memset", mx2.ap[:, :], 0.0, writes=[mx2],
                 sem="vv")
            for ht, (h0, wdt) in enumerate(HT):
                for dram, tiles, eng, sem in ((wgT_q, wg_t, "sync", "sh"),
                                              (wuT_q, wu_t, "scalar", "sh2")):
                    items = []
                    for half in range(2):
                        outv = bfv(tiles[half])[:, 0:8 * 512].rearrange(
                            "p (dc c) -> p dc c", c=512)[:, :, 0:wdt]
                        inv = dram[half * 1024:(half + 1) * 1024,
                                   h0:h0 + wdt].rearrange(
                            "(dc p) c -> p dc c", p=128)
                        items.append((outv, inv))
                    k.dma_batch(eng, sem, items,
                                writes=[x_tm[tiles[0]], x_tm[tiles[1]]],
                                extra=[v_qcache])
                for tc in range(2):
                    for pi, (dram, tiles) in enumerate(
                            ((wgT_q, wg_t), (wuT_q, wu_t))):
                        bank = pb[pi]
                        grp_reads = [x_tm[nT_t], x_tm[tiles[0]],
                                     x_tm[tiles[1]]]
                        for dc in range(16):
                            til = tiles[dc // 8]
                            lhs = bfv(nT_t)[:, dc * 256 + tc * 128:
                                            dc * 256 + (tc + 1) * 128]
                            rhs = bfv(til)[:, (dc % 8) * 512:
                                           (dc % 8) * 512 + wdt]
                            k.do("tensor", "matmul", bank.ap[:, 0:wdt], lhs,
                                 rhs, start=(dc == 0), stop=(dc == 15),
                                 reads=(grp_reads if dc == 15
                                        else [x_tm[nT_t], x_tm[til]]),
                                 writes=[bank],
                                 sem="mm" if dc == 15 else None)
                    k.do("scalar", "activation", out=t_a.ap[:, 0:wdt],
                         in_=pb[0].ap[:, 0:wdt], func=AF.Silu,
                         scale=rx.ap[:, 0 + tc:1 + tc], reads=[pb[0], rx],
                         writes=[t_a], sem="aa")
                    k.do("vector", "tensor_scalar", t_b.ap[:, 0:wdt],
                         pb[1].ap[:, 0:wdt], rx.ap[:, 2 + tc:3 + tc], None,
                         ALU.mult, reads=[pb[1], rx], writes=[t_b], sem="vv")
                    if g == 0 and ht == 0 and tc == 0:
                        dbg_cp(1024, t_a.ap[:, 0:512], t_a, 512)
                        dbg_cp(1536, t_b.ap[:, 0:512], t_b, 512)
                    gut = x_tm[gu_t[tc][ht // 4]]
                    gcol = (ht % 4) * 512
                    k.do("vector", "tensor_mul",
                         gut.ap[:, gcol:gcol + wdt], t_a.ap[:, 0:wdt],
                         t_b.ap[:, 0:wdt], reads=[t_a, t_b], writes=[gut],
                         sem="vv")
                    if g == 0 and ht == 0 and tc == 0:
                        dbg_cp(2048, gut.ap[:, 0:512], gut, 512)
                    k.do("vector", "reduce_max", rx.ap[:, 4 + tc:5 + tc],
                         gut.ap[:, gcol:gcol + wdt], axis=AXX,
                         apply_absolute_value=True, reads=[gut], writes=[rx],
                         sem="vv")
                    k.do("vector", "tensor_max", mx2.ap[:, tc:tc + 1],
                         mx2.ap[:, tc:tc + 1], rx.ap[:, 4 + tc:5 + tc],
                         reads=[rx], writes=[mx2], sem="vv")

            for tc in range(2):
                k.do("vector", "tensor_scalar", rx.ap[:, 8 + tc:9 + tc],
                     mx2.ap[:, tc:tc + 1], EPS, 1.0 / 127.0, ALU.max,
                     ALU.mult, reads=[mx2], writes=[rx], sem="vv")
                k.do("vector", "tensor_mul", rx.ap[:, 6 + tc:7 + tc],
                     rx.ap[:, 8 + tc:9 + tc], scq.ap[:, 5:6], reads=[rx, scq],
                     writes=[rx], sem="vv")
                k.do("vector", "reciprocal", rx.ap[:, 10 + tc:11 + tc],
                     rx.ap[:, 8 + tc:9 + tc], reads=[rx], writes=[rx],
                     sem="vv")
                for ht, (h0, wdt) in enumerate(HT):
                    gut = x_tm[gu_t[tc][ht // 4]]
                    gcol = (ht % 4) * 512
                    nti = n2bf_t[h0 // 4096]
                    ncol = h0 % 4096
                    quant_chain("vector", gut,
                                bfv(nti)[:, ncol:ncol + wdt], 128, wdt,
                                rx.ap[:, 10 + tc:11 + tc], -128.0, 127.0,
                                x_tm[nti], src_c0=gcol)
                if g == 0 and tc == 0:
                    dbg_cp(24, rx.ap[:, 0:12], rx, 12)
                    dbg_cp(2560, bfv(n2bf_t[0])[:, 0:512], x_tm[n2bf_t[0]],
                           512)
                for hcg in range(13):
                    bank = pb[4]
                    nt = min(4, NC0 - hcg * 4)
                    grp_reads = [x_tm[n2bf_t[((hcg * 4 + q) * 128) // 4096]]
                                 for q in range(nt)] + [ident_b]
                    for q in range(nt):
                        hc = hcg * 4 + q
                        nti = n2bf_t[(hc * 128) // 4096]
                        ncol = (hc * 128) % 4096
                        k.do("tensor", "matmul",
                             bank.ap[:, q * 128:(q + 1) * 128],
                             bfv(nti)[:, ncol:ncol + 128], ident_b.ap[:, :],
                             start=True, stop=True,
                             reads=(grp_reads if q == nt - 1
                                    else [x_tm[nti], ident_b]),
                             writes=[bank],
                             sem="mm" if q == nt - 1 else None)
                    for q in range(nt):
                        hc = hcg * 4 + q
                        dst = n2T_t[hc // 16]
                        dcol = (hc % 16) * 256 + tc * 128
                        k.do("scalar", "activation",
                             out=bfv(dst)[:, dcol:dcol + 128],
                             in_=bank.ap[:, q * 128:(q + 1) * 128],
                             func=AF.Identity, reads=[bank],
                             writes=[x_tm[dst]], sem="aa")

            k.dma_batch("sync", "sx",
                        [(x_tm[xr_t[tc]].ap[:, :],
                          spill_x[G[tc] * 128:(G[tc] + 1) * 128, :])
                         for tc in range(2)],
                        writes=[x_tm[xr_t[0]], x_tm[xr_t[1]]],
                        extra=[v_spill])
            for et in range(4):
                ecols = slice(et * 512, (et + 1) * 512)
                items = {"sync": [], "scalar": []}
                wr = {"sync": [], "scalar": []}
                for ti in range(7):
                    h0 = ti * 1024
                    nb = min(8, NC0 - ti * 8)
                    if nb <= 0:
                        break
                    outv = bfv(dn_t[ti])[:, 0:nb * 512].rearrange(
                        "p (b c) -> p b c", c=512)
                    inv = wdT_q[h0:h0 + nb * 128, ecols].rearrange(
                        "(b p) c -> p b c", p=128)
                    eng = "sync" if ti < 4 else "scalar"
                    items[eng].append((outv, inv))
                    wr[eng].append(x_tm[dn_t[ti]])
                k.dma_batch("sync", "sh", items["sync"], writes=wr["sync"],
                            extra=[v_qcache])
                k.dma_batch("scalar", "sh2", items["scalar"],
                            writes=wr["scalar"], extra=[v_qcache])
                if g == 0 and et == 0:
                    dbg_cp(3584, bfv(n2T_t[0])[:, 0:512], x_tm[n2T_t[0]], 512)
                    dbg_cp(4096, bfv(dn_t[0])[:, 0:512], x_tm[dn_t[0]], 512)
                for tc in range(2):
                    bank = pb[2 + tc]
                    grp_reads = ([x_tm[t] for t in n2T_t]
                                 + [x_tm[t] for t in dn_t])
                    for hc in range(NC0):
                        lhs = bfv(n2T_t[hc // 16])[:, (hc % 16) * 256 +
                                                   tc * 128:
                                                   (hc % 16) * 256 +
                                                   (tc + 1) * 128]
                        rhs = bfv(dn_t[hc // 8])[:, (hc % 8) * 512:
                                                 (hc % 8) * 512 + 512]
                        k.do("tensor", "matmul", bank.ap[:, :], lhs, rhs,
                             start=(hc == 0), stop=(hc == NC0 - 1),
                             reads=(grp_reads if hc == NC0 - 1
                                    else [x_tm[n2T_t[hc // 16]],
                                          x_tm[dn_t[hc // 8]]]),
                             writes=[bank],
                             sem="mm" if hc == NC0 - 1 else None)
                    xr = x_tm[xr_t[tc]]
                    if g == 0 and et == 0 and tc == 0:
                        dbg_cp(3072, bank.ap[:, :], bank, 512, psum=True)
                    k.do("vector", "scalar_tensor_tensor",
                         xr.ap[:, ecols], bank.ap[:, :],
                         rx.ap[:, 6 + tc:7 + tc], xr.ap[:, ecols],
                         ALU.mult, ALU.add, reads=[bank, rx], writes=[xr],
                         sem="vv")
            _layernorm(k, [x_tm[xr_t[0]], x_tm[xr_t[1]]], msc, t_a, sq)
            k.dma_batch("gpsimd", "so",
                        [(spill_x[G[tc] * 128:(G[tc] + 1) * 128, :],
                          x_tm[xr_t[tc]].ap[:, :]) for tc in range(2)],
                        reads=[x_tm[xr_t[0]], x_tm[xr_t[1]]])

        k.dma_batch("sync", "sx",
                    [(x_tm[i].ap[:, :], spill_x[i * 128:(i + 1) * 128, :])
                     for i in range(NTC)], writes=x_tm,
                    extra=[("so", k.val("so"))])


def _stage_l1_attn(cfg):
    k = cfg["k"]; nc = cfg["nc"]; W = cfg["W"]; C = cfg["C"]
    x_tm = cfg["x_tm"]; pb = cfg["pb"]; co = cfg["consts"]
    ident_b = co["ident_b"]; ident_f = co["ident_f"]
    flag = co["flag"]; flagi = co["flagi"]
    xt_h = cfg["xt_h"]; xt_l = cfg["xt_l"]; spill_x = cfg["spill_x"]
    cc1_in = cfg["cc1_in"]; cc1_out = cfg["cc1_out"]; PAIRS = cfg["PAIRS"]

    with contextlib.ExitStack() as st:
        def sbuf(name, shape, dt=F32):
            return Res(st.enter_context(nc.sbuf_tensor(name, shape, dt)))

        wld = sbuf("a1_wld", [128, 1024])
        w_h = sbuf("a1_wh", [128, D], BF16)
        w_l = sbuf("a1_wl", [128, D], BF16)
        xtb2 = [sbuf(f"a1_xtb{i}", [128, 2 * 512], BF16) for i in range(2)]
        g1 = sbuf("a1_g1", [128, T])
        kbuf = sbuf("a1_kbuf", [128, T])
        vbuf = sbuf("a1_vbuf", [128, T])
        qbuf = sbuf("a1_qbuf", [128, T])
        obuf = sbuf("a1_obuf", [128, T])
        onT_h = sbuf("a1_onTh", [128, T], BF16)
        onT_l = sbuf("a1_onTl", [128, T], BF16)
        cosb = sbuf("a1_cosb", [128, 512])
        sinb = sbuf("a1_sinb", [128, 512])
        mrot = sbuf("a1_mrot", [128, 128])
        dmk = sbuf("a1_dmk", [128, 128])
        nwj = sbuf("a1_nwj", [128, 128])
        deck = sbuf("a1_deck", [128, NH])
        decq = sbuf("a1_decq", [128, NH])
        s0p = sbuf("a1_s0p", [128, 128])
        s0c = [sbuf(f"a1_s0c{i}", [128, 128]) for i in range(2)]
        sct = [sbuf(f"a1_sct{i}", [128, 128]) for i in range(2)]
        t1 = sbuf("a1_t1", [128, 512])
        t2 = sbuf("a1_t2", [128, 512])
        colt = sbuf("a1_colt", [128, 4])
        msc = sbuf("a1_msc", [128, 2 * NTC])
        sq = sbuf("a1_sq", [128, 1])
        k.fence()

        k.dma("sync", deck.ap[:, :], C["deck"], "sw", writes=[deck])
        k.dma("sync", decq.ap[:, :], C["decq"], "sw", writes=[decq])
        k.dma("sync", mrot.ap[:, :], C["mrot"], "sw", writes=[mrot])

        loglam = [-(2.0 ** (-8.0 * (j + 1.0) / NH)) for j in range(NH)]
        lamC = [float(np.exp(loglam[j] * CH)) for j in range(NH)]

        xt_il = cfg["xt_il"]
        for tc in range(NTC):
            _split(k, x_tm[tc], w_h, w_l)
            for dcg in range(8):
                bank = pb[4]
                for qq in range(4):
                    dc = dcg * 2 + qq // 2
                    wsrc = w_h if qq % 2 == 0 else w_l
                    k.do("tensor", "matmul",
                         bank.ap[:, qq * 128:(qq + 1) * 128],
                         wsrc.ap[:, dc * 128:(dc + 1) * 128],
                         ident_b.ap[:, :], start=True, stop=True,
                         reads=[w_h, w_l, ident_b], writes=[bank],
                         sem="mm" if qq == 3 else None)
                k.do("scalar", "activation",
                     out=t1.ap[:, 0:256].bitcast(BF16),
                     in_=bank.ap[:, :], func=AF.Identity, reads=[bank],
                     writes=[t1], sem="aa")
                dst = xt_il[dcg * 512:(dcg + 1) * 512,
                            tc * 128:(tc + 1) * 128].rearrange(
                    "(b p) c -> p b c", p=128)
                k.dma("gpsimd", dst,
                      t1.ap[:, 0:256].bitcast(BF16).rearrange(
                          "p (b c) -> p b c", c=128), "so", reads=[t1])
        v_xt = ("so", k.val("so"))

        def load_xtb(tau, quarter, par):
            xtb = xtb2[par]
            c0 = tau * 512
            r0 = quarter * 256
            k.dma_batch("sync" if par == 0 else "scalar",
                        "sh" if par == 0 else "sh2",
                        [(xtb.ap[:, :].rearrange("p (b c) -> p b c", c=512),
                          xt_il[r0:r0 + 256, c0:c0 + 512].rearrange(
                              "(b p) c -> p b c", p=128))],
                        writes=[xtb], extra=[v_xt])

        def load_wcol(wap, j):
            for half in range(2):
                sl = wap[half * 1024:(half + 1) * 1024,
                         j * 128:(j + 1) * 128].rearrange(
                    "(dc p) c -> p dc c", p=128)
                k.dma("sync", wld.ap[:, 0:1024].rearrange(
                    "p (dc c) -> p dc c", c=128), sl, "sw", writes=[wld])
                _split(k, wld, w_h, w_l, cols=slice(half * 1024,
                                                    (half + 1) * 1024),
                       src_cols=slice(0, 1024))
            return

        def proj_pass(dst, silu, bias_dq=False):
            for tau in range(4):
                bank = pb[tau % 2]
                first = True
                for quarter in range(16):
                    par = (tau * 16 + quarter) % 2
                    load_xtb(tau, quarter, par)
                    xtb = xtb2[par]
                    dc = quarter
                    lh = w_h.ap[:, dc * 128:(dc + 1) * 128]
                    ll = w_l.ap[:, dc * 128:(dc + 1) * 128]
                    rh = xtb.ap[:, 0:512]
                    rl = xtb.ap[:, 512:1024]
                    last = (quarter == 15)
                    k.do("tensor", "matmul", bank.ap[:, :], lh, rh,
                         start=first, stop=False, reads=[w_h, xtb],
                         writes=[bank])
                    first = False
                    k.do("tensor", "matmul", bank.ap[:, :], lh, rl,
                         start=False, stop=False, reads=[w_h, xtb],
                         writes=[bank])
                    k.do("tensor", "matmul", bank.ap[:, :], ll, rh,
                         start=False, stop=last,
                         reads=[w_h, w_l, xtb], writes=[bank], sem="mm")
                cols = slice(tau * 512, (tau + 1) * 512)
                k.do("scalar", "activation", out=dst.ap[:, cols],
                     in_=bank.ap[:, :], func=AF.Silu if silu else AF.Identity,
                     reads=[bank], writes=[dst], sem="aa")

        def rotate(dst):
            for tau in range(4):
                cols = slice(tau * 512, (tau + 1) * 512)
                k.dma_batch("sync", "sx",
                            [(cosb.ap[:, :], C["cs_t"][:, cols]),
                             (sinb.ap[:, :], C["sc_t"][:, cols])],
                            writes=[cosb, sinb])
                k.do("vector", "tensor_mul", t1.ap[0:64, :],
                     dst.ap[0:64, cols], cosb.ap[0:64, :],
                     reads=[dst, cosb], writes=[t1], sem="vv")
                k.do("vector", "tensor_mul", t1.ap[64:128, :],
                     dst.ap[64:128, cols], cosb.ap[64:128, :],
                     reads=[dst, cosb], writes=[t1], sem="vv")
                k.do("vector", "tensor_mul", t2.ap[0:64, :],
                     dst.ap[0:64, cols], sinb.ap[0:64, :],
                     reads=[dst, sinb], writes=[t2], sem="vv")
                k.do("vector", "tensor_mul", t2.ap[64:128, :],
                     dst.ap[64:128, cols], sinb.ap[64:128, :],
                     reads=[dst, sinb], writes=[t2], sem="vv")
                bank = pb[2]
                k.do("tensor", "matmul", bank.ap[0:64, :],
                     mrot.ap[:, 0:64], t1.ap[:, :], start=True, stop=True,
                     reads=[mrot, t1], writes=[bank], sem="mm")
                k.do("tensor", "matmul", bank.ap[64:128, :],
                     mrot.ap[:, 64:128], t2.ap[:, :], start=True, stop=True,
                     tile_position=(0, 64), reads=[mrot, t2], writes=[bank],
                     sem="mm")
                k.do("scalar", "activation", out=dst.ap[:, cols],
                     in_=bank.ap[:, :], func=AF.Identity, reads=[bank],
                     writes=[dst], sem="aa")

        load_wcol(W["l1_wg1"], 0)
        for tau in range(4):
            bank = pb[2]
            first = True
            for quarter in range(16):
                par = (tau * 16 + quarter) % 2
                load_xtb(tau, quarter, par)
                xtb = xtb2[par]
                dc = quarter
                rh = xtb.ap[:, 0:512]
                rl = xtb.ap[:, 512:1024]
                last = (quarter == 15)
                k.do("tensor", "matmul", bank.ap[:, :],
                     w_h.ap[:, dc * 128:(dc + 1) * 128], rh,
                     start=first, stop=False,
                     reads=[w_h, xtb], writes=[bank])
                first = False
                k.do("tensor", "matmul", bank.ap[:, :],
                     w_h.ap[:, dc * 128:(dc + 1) * 128], rl,
                     start=False, stop=False,
                     reads=[w_h, xtb], writes=[bank])
                k.do("tensor", "matmul", bank.ap[:, :],
                     w_l.ap[:, dc * 128:(dc + 1) * 128], rh,
                     start=False, stop=last,
                     reads=[w_h, w_l, xtb], writes=[bank], sem="mm")
            k.do("scalar", "activation", out=g1.ap[:, tau * 512:(tau + 1) * 512],
                 in_=bank.ap[:, :], func=AF.Identity, reads=[bank],
                 writes=[g1], sem="aa")

        for j in range(NH):
            jc = slice(j * 128, (j + 1) * 128)
            load_wcol(W["l1_wk"], j)
            proj_pass(kbuf, silu=True)
            rotate(kbuf)
            load_wcol(W["l1_wv"], j)
            proj_pass(qbuf, silu=False)
            for c in range(16):
                bank = pb[4]
                k.do("tensor", "transpose", bank.ap[:, 0:128],
                     qbuf.ap[:, c * 128:(c + 1) * 128], ident_f.ap[:, :],
                     reads=[qbuf, ident_f], writes=[bank], sem="mm")
                k.do("scalar", "activation",
                     out=vbuf.ap[:, c * 128:(c + 1) * 128],
                     in_=bank.ap[:, 0:128], func=AF.Identity, reads=[bank],
                     writes=[vbuf], sem="aa")
            load_wcol(W["l1_wq"], j)
            proj_pass(qbuf, silu=True)
            rotate(qbuf)
            k.dma_batch("sync", "sx",
                        [(dmk.ap[:, :], C["dmask"][:, jc]),
                         (nwj.ap[:, :], C["nw_rep"][:, jc])],
                        writes=[dmk, nwj])
            for c in range(16):
                cc = slice(c * 128, (c + 1) * 128)
                pa = c % 2
                bs = pb[2 + 4 * pa]
                bo_ = pb[3 + 4 * pa]
                bt = pb[4 + pa]
                k.do("tensor", "matmul", bs.ap[:, 0:128],
                     kbuf.ap[:, cc], qbuf.ap[:, cc], start=True, stop=True,
                     reads=[kbuf, qbuf], writes=[bs], sem="mm")
                k.do("vector", "tensor_mul", sct[pa].ap[:, :],
                     bs.ap[:, 0:128], dmk.ap[:, :], reads=[bs, dmk],
                     writes=[sct[pa]], sem="vv")
                k.do("tensor", "transpose", bt.ap[:, 0:128],
                     kbuf.ap[:, cc], ident_f.ap[:, :],
                     reads=[kbuf, ident_f], writes=[bt], sem="mm")
                k.do("vector", "tensor_scalar", kbuf.ap[:, cc],
                     bt.ap[:, 0:128], deck.ap[:, j:j + 1], None, ALU.mult,
                     reads=[bt, deck], writes=[kbuf], sem="vv")
                k.do("tensor", "matmul", bo_.ap[:, 0:128], sct[pa].ap[:, :],
                     vbuf.ap[:, cc], start=True, stop=True,
                     reads=[sct[pa], vbuf], writes=[bo_], sem="mm")
                k.do("scalar", "activation", out=obuf.ap[:, cc],
                     in_=bo_.ap[:, 0:128], func=AF.Identity,
                     reads=[bo_], writes=[obuf], sem="aa")
                k.do("tensor", "matmul", bt.ap[:, 128:256],
                     kbuf.ap[:, cc], vbuf.ap[:, cc], start=True, stop=True,
                     reads=[kbuf, vbuf], writes=[bt], sem="mm")
                if c == 0:
                    k.do("vector", "tensor_scalar", vbuf.ap[:, cc],
                         bt.ap[:, 128:256], 1.0, None, ALU.mult,
                         reads=[bt], writes=[vbuf], sem="vv")
                else:
                    k.do("vector", "scalar_tensor_tensor", vbuf.ap[:, cc],
                         vbuf.ap[:, (c - 1) * 128:c * 128], lamC[j],
                         bt.ap[:, 128:256], ALU.mult, ALU.add,
                         reads=[vbuf, bt], writes=[vbuf], sem="vv")
            k.do("gpsimd", "tensor_scalar", t1.ap[:, 0:128],
                 vbuf.ap[:, 15 * 128:16 * 128], flagi.ap[:, 0:1], None,
                 ALU.mult, reads=[vbuf, flagi], writes=[t1], sem="gg")
            vz = k.dma("gpsimd", cc1_in[j][:, :], t1.ap[:, 0:128], "sc",
                       reads=[t1])
            k.do("gpsimd", "collective_compute", "AllReduce", ALU.add,
                 PAIRS, [cc1_in[j][:, :]], [cc1_out[j][:, :]], sem="cc",
                 extra=[vz])
            k.dma("gpsimd", s0p.ap[:, :], cc1_out[j][:, :], "sc",
                  writes=[s0p], extra=[("cc", k.val("cc"))])
            k.do("gpsimd", "tensor_scalar", s0p.ap[:, :], s0p.ap[:, :],
                 flag.ap[:, 0:1], None, ALU.mult, reads=[flag], writes=[s0p],
                 sem="gg")
            for c in range(16):
                cc = slice(c * 128, (c + 1) * 128)
                pa = c % 2
                bo_ = pb[3 + 4 * pa]
                lam_p = float(np.exp(loglam[j] * CH * c))
                k.do("vector", "tensor_scalar", s0c[pa].ap[:, :],
                     s0p.ap[:, :], lam_p, None, ALU.mult, reads=[s0p],
                     writes=[s0c[pa]], sem="vv")
                if c == 0:
                    k.do("tensor", "matmul", bo_.ap[:, 0:128],
                         qbuf.ap[:, cc], s0c[pa].ap[:, :], start=True,
                         stop=True, reads=[qbuf, s0c[pa]], writes=[bo_],
                         sem="mm")
                else:
                    k.do("tensor", "matmul", bo_.ap[:, 0:128],
                         qbuf.ap[:, cc], vbuf.ap[:, (c - 1) * 128:c * 128],
                         start=True, stop=False, reads=[qbuf, vbuf],
                         writes=[bo_])
                    k.do("tensor", "matmul", bo_.ap[:, 0:128],
                         qbuf.ap[:, cc], s0c[pa].ap[:, :], start=False,
                         stop=True, reads=[qbuf, vbuf, s0c[pa]],
                         writes=[bo_], sem="mm")
                k.do("vector", "scalar_tensor_tensor", obuf.ap[:, cc],
                     bo_.ap[:, 0:128], decq.ap[:, j:j + 1],
                     obuf.ap[:, cc], ALU.mult, ALU.add,
                     reads=[bo_, decq], writes=[obuf], sem="vv")
            slot = 0
            k.dma("sync", wld.ap[:, 0:128],
                  W["l1_wg2"][:, jc], "sw", writes=[wld])
            for c in range(16):
                cc = slice(c * 128, (c + 1) * 128)
                k.do("tensor", "matmul", pb[5].ap[:, 0:128],
                     g1.ap[:, cc], wld.ap[:, 0:128], start=True, stop=True,
                     reads=[g1, wld], writes=[pb[5]], sem="mm")
                k.do("scalar", "activation", out=t1.ap[:, 0:128],
                     in_=pb[5].ap[:, 0:128], func=AF.Sigmoid,
                     reads=[pb[5]], writes=[t1], sem="aa")
                k.do("vector", "tensor_mul", t1.ap[:, 0:128],
                     obuf.ap[:, cc], t1.ap[:, 0:128], reads=[obuf, t1],
                     writes=[t1], sem="vv")
                k.do("vector", "tensor_mul", t2.ap[:, 0:128],
                     t1.ap[:, 0:128], t1.ap[:, 0:128], reads=[t1],
                     writes=[t2], sem="vv")
                k.do("vector", "tensor_reduce", colt.ap[:, 0:1],
                     t2.ap[:, 0:128], axis=AXX, op=ALU.add, reads=[t2],
                     writes=[colt], sem="vv")
                k.do("vector", "tensor_scalar", colt.ap[:, 1:2],
                     colt.ap[:, 0:1], 1.0 / 128.0, 1e-6, ALU.mult, ALU.add,
                     reads=[colt], writes=[colt], sem="vv")
                k.do("scalar", "activation", out=colt.ap[:, 2:3],
                     in_=colt.ap[:, 1:2], func=AF.Sqrt, reads=[colt],
                     writes=[colt], sem="aa")
                k.do("vector", "reciprocal", colt.ap[:, 3:4],
                     colt.ap[:, 2:3], reads=[colt], writes=[colt], sem="vv")
                k.do("vector", "tensor_scalar", t1.ap[:, 0:128],
                     t1.ap[:, 0:128], colt.ap[:, 3:4], None, ALU.mult,
                     reads=[colt], writes=[t1], sem="vv")
                k.do("vector", "tensor_mul", t1.ap[:, 0:128],
                     t1.ap[:, 0:128], nwj.ap[:, :], reads=[nwj],
                     writes=[t1], sem="vv")
                k.do("scalar", "activation",
                     out=t2.ap[:, 0:64].bitcast(BF16),
                     in_=t1.ap[:, 0:128], func=AF.Identity, reads=[t1],
                     writes=[t2], sem="aa")
                k.do("vector", "tensor_sub",
                     t2.ap[:, 64:128].bitcast(BF16),
                     t1.ap[:, 0:128], t2.ap[:, 0:64].bitcast(BF16),
                     reads=[t1], writes=[t2], sem="vv")
                for hl, dstb in ((0, onT_h), (1, onT_l)):
                    srcap = (t2.ap[:, 0:64].bitcast(BF16) if hl == 0
                             else t2.ap[:, 64:128].bitcast(BF16))
                    k.do("tensor", "matmul", pb[4].ap[:, 0:128],
                         srcap, ident_b.ap[:, :], start=True, stop=True,
                         reads=[t2, ident_b], writes=[pb[4]], sem="mm")
                    k.do("scalar", "activation",
                         out=dstb.ap[:, c * 128:(c + 1) * 128],
                         in_=pb[4].ap[:, 0:128], func=AF.Identity,
                         reads=[pb[4]], writes=[dstb], sem="aa")
            if True:
                for s2 in range(1):
                    jo = j
                    k.dma("sync", wld.ap[:, 0:1024],
                          W["l1_wo"][jo * 128:(jo + 1) * 128, 0:1024], "sw",
                          writes=[wld])
                    _split(k, wld, w_h, w_l, cols=slice(0, 1024),
                           src_cols=slice(0, 1024))
                    k.dma("sync", wld.ap[:, 0:1024],
                          W["l1_wo"][jo * 128:(jo + 1) * 128, 1024:2048],
                          "sw", writes=[wld])
                    _split(k, wld, w_h, w_l, cols=slice(1024, 2048),
                           src_cols=slice(0, 1024))
                    for et in range(4):
                        ecols = slice(et * 512, (et + 1) * 512)
                        for tc in range(NTC):
                            bank = pb[6 + tc % 2]
                            ch = onT_h.ap[:, tc * 128:(tc + 1) * 128]
                            cl = onT_l.ap[:, tc * 128:(tc + 1) * 128]
                            rh = w_h.ap[:, ecols]
                            rl = w_l.ap[:, ecols]
                            k.do("tensor", "matmul", bank.ap[:, :], ch, rh,
                                 start=True, stop=False,
                                 reads=[onT_h, w_h], writes=[bank])
                            k.do("tensor", "matmul", bank.ap[:, :], ch, rl,
                                 start=False, stop=False,
                                 reads=[onT_h, w_l], writes=[bank])
                            k.do("tensor", "matmul", bank.ap[:, :], cl, rh,
                                 start=False, stop=True,
                                 reads=[onT_h, onT_l, w_h, w_l],
                                 writes=[bank], sem="mm")
                            k.do("vector", "tensor_add",
                                 x_tm[tc].ap[:, ecols], x_tm[tc].ap[:, ecols],
                                 bank.ap[:, :], reads=[bank],
                                 writes=[x_tm[tc]], sem="vv")

        _layernorm(k, x_tm, msc, qbuf, sq)


def _stage_l1_ffn(cfg):
    k = cfg["k"]; nc = cfg["nc"]; W = cfg["W"]; C = cfg["C"]
    x_tm = cfg["x_tm"]; pb = cfg["pb"]; co = cfg["consts"]
    ident_b = co["ident_b"]; ones_f = co["ones_f"]
    xt_h = cfg["xt_h"]; xt_l = cfg["xt_l"]; spill_x = cfg["spill_x"]
    w1T_q = cfg["w1T_q"]; w3T_q = cfg["w3T_q"]; w2T_q = cfg["w2T_q"]
    HT = [(i * 512, 512) for i in range(10)] + [(5120, 384)]

    with contextlib.ExitStack() as st:
        def sbuf(name, shape, dt=F32):
            return Res(st.enter_context(nc.sbuf_tensor(name, shape, dt)))

        w_f = sbuf("f1_wf", [128, D])
        wq_b = sbuf("f1_wqb", [128, D], BF16)
        tx_b = sbuf("f1_txb", [128, D], BF16)
        t_a = sbuf("f1_ta", [128, D])
        t_b = sbuf("f1_tb", [128, D])
        i_t = sbuf("f1_it", [128, D], mybir.dt.int32)
        scq = sbuf("f1_scq", [128, 8])
        acc = sbuf("f1_acc", [128, 2])
        sm = sbuf("f1_sm", [1, 8])
        msc = sbuf("f1_msc", [128, 8])
        sq = sbuf("f1_sq", [128, 1])
        k.fence()

        def bfv(ti):
            return x_tm[ti].ap.bitcast(BF16)

        xt_il = cfg["xt_il"]
        for tc in range(NTC):
            _split(k, x_tm[tc], wq_b, tx_b)
            for dcg in range(8):
                bank = pb[4]
                for qq in range(4):
                    dc = dcg * 2 + qq // 2
                    wsrc = wq_b if qq % 2 == 0 else tx_b
                    k.do("tensor", "matmul",
                         bank.ap[:, qq * 128:(qq + 1) * 128],
                         wsrc.ap[:, dc * 128:(dc + 1) * 128],
                         ident_b.ap[:, :], start=True, stop=True,
                         reads=[wq_b, tx_b, ident_b], writes=[bank],
                         sem="mm" if qq == 3 else None)
                k.do("scalar", "activation",
                     out=t_a.ap[:, 0:256].bitcast(BF16),
                     in_=bank.ap[:, :], func=AF.Identity, reads=[bank],
                     writes=[t_a], sem="aa")
                dst = xt_il[dcg * 512:(dcg + 1) * 512,
                            tc * 128:(tc + 1) * 128].rearrange(
                    "(b p) c -> p b c", p=128)
                k.dma("gpsimd", dst,
                      t_a.ap[:, 0:256].bitcast(BF16).rearrange(
                          "p (b c) -> p b c", c=128), "so", reads=[t_a])
        k.dma_batch("gpsimd", "so",
                    [(spill_x[i * 128:(i + 1) * 128, :], x_tm[i].ap[:, :])
                     for i in range(NTC)], reads=x_tm)
        v_spill = ("so", k.val("so"))

        def wchunks(nm):
            if nm in ("l1_w1", "l1_w3"):
                for r0 in range(0, I1, 128):
                    nr = min(128, I1 - r0)
                    yield W[nm][r0:r0 + nr, :], nr, D
            else:
                for r0 in range(0, D, 128):
                    for c0 in range(0, I1, 2048):
                        w = min(2048, I1 - c0)
                        yield W[nm][r0:r0 + 128, c0:c0 + w], 128, w

        for mi, nm in enumerate(["l1_w1", "l1_w3", "l1_w2"]):
            k.do("vector", "memset", acc.ap[:, 0:2], 0.0, writes=[acc],
                 sem="vv")
            for sl, nr, wcols in wchunks(nm):
                k.dma("sync", w_f.ap[0:nr, 0:wcols], sl, "sw", writes=[w_f])
                k.do("scalar", "activation", out=t_a.ap[0:nr, 0:wcols],
                     in_=w_f.ap[0:nr, 0:wcols], func=AF.Abs, reads=[w_f],
                     writes=[t_a], sem="aa")
                k.do("vector", "tensor_reduce", acc.ap[0:nr, 1:2],
                     t_a.ap[0:nr, 0:wcols], axis=AXX, op=ALU.add,
                     reads=[t_a], writes=[acc], sem="vv")
                k.do("vector", "tensor_add", acc.ap[0:nr, 0:1],
                     acc.ap[0:nr, 0:1], acc.ap[0:nr, 1:2], reads=[acc],
                     writes=[acc], sem="vv")
            k.do("tensor", "matmul", pb[5].ap[0:1, 0:1], acc.ap[:, 0:1],
                 ones_f.ap[:, 0:1], start=True, stop=True,
                 reads=[acc, ones_f], writes=[pb[5]], sem="mm")
            k.do("vector", "tensor_scalar", sm.ap[0:1, 0:1],
                 pb[5].ap[0:1, 0:1], 1.0 / (I1 * D), None, ALU.mult,
                 reads=[pb[5]], writes=[sm], sem="vv")
            k.do("vector", "reciprocal", sm.ap[0:1, 1:2], sm.ap[0:1, 0:1],
                 reads=[sm], writes=[sm], sem="vv")
            k.do("tensor", "matmul", pb[5].ap[:, 0:2], ones_f.ap[0:1, 0:128],
                 sm.ap[0:1, 0:2], start=True, stop=True, reads=[ones_f, sm],
                 writes=[pb[5]], sem="mm")
            k.do("vector", "tensor_copy", scq.ap[:, 2 * mi:2 * mi + 1],
                 pb[5].ap[:, 1:2], reads=[pb[5]], writes=[scq], sem="vv")
            k.do("vector", "tensor_copy", scq.ap[:, 2 * mi + 1:2 * mi + 2],
                 pb[5].ap[:, 0:1], reads=[pb[5]], writes=[scq], sem="vv")

        def quant_chain(src, dst_ap, nr, wcols, sc_col, dst_res, src_c0=0):
            k.do("vector", "tensor_scalar", t_a.ap[0:nr, 0:wcols],
                 src.ap[0:nr, src_c0:src_c0 + wcols], sc_col[0:nr, :], 0.5,
                 ALU.mult, ALU.add, reads=[src, scq], writes=[t_a], sem="vv")
            k.do("vector", "tensor_scalar_add", i_t.ap[0:nr, 0:wcols],
                 t_a.ap[0:nr, 0:wcols], 0.0, reads=[t_a], writes=[i_t],
                 sem="vv")
            k.do("vector", "tensor_tensor", t_b.ap[0:nr, 0:wcols],
                 i_t.ap[0:nr, 0:wcols], t_a.ap[0:nr, 0:wcols], ALU.is_gt,
                 reads=[i_t, t_a], writes=[t_b], sem="vv")
            k.do("vector", "tensor_sub", t_a.ap[0:nr, 0:wcols],
                 i_t.ap[0:nr, 0:wcols], t_b.ap[0:nr, 0:wcols],
                 reads=[i_t, t_b], writes=[t_a], sem="vv")
            k.do("vector", "tensor_scalar", dst_ap, t_a.ap[0:nr, 0:wcols],
                 1.0, -1.0, ALU.min, ALU.max, reads=[t_a],
                 writes=[dst_res], sem="vv")

        for nm, dram, sc_i in [("l1_w1", w1T_q, 0), ("l1_w3", w3T_q, 2)]:
            for ci, r0 in enumerate(range(0, I1P, 128)):
                nr = min(128, I1 - r0) if r0 < I1 else 0
                if nr < 128:
                    k.do("vector", "memset", wq_b.ap[:, :], 0.0,
                         writes=[wq_b], sem="vv")
                if nr > 0:
                    k.dma("sync", w_f.ap[0:nr, :], W[nm][r0:r0 + nr, :], "sw",
                          writes=[w_f])
                    quant_chain(w_f, wq_b.ap[0:nr, :], nr, D,
                                scq.ap[:, sc_i:sc_i + 1], wq_b)
                _transpose16(k, pb, ident_b, wq_b, tx_b, n=16)
                out = dram[0:D, ci * 128:(ci + 1) * 128].rearrange(
                    "(dc p) c -> p dc c", p=128)
                in_ = tx_b.ap[:, :].rearrange("p (dc c) -> p dc c", c=128)
                k.dma("gpsimd", out, in_, "sq", reads=[tx_b])
        for ci, r0 in enumerate(range(0, D, 128)):
            for pi, c0 in enumerate(range(0, I1P, 2048)):
                wreal = max(0, min(2048, I1 - c0))
                wpad = min(2048, I1P - c0)
                ntile = wpad // 128
                if wreal < wpad:
                    k.do("vector", "memset", wq_b.ap[:, wreal:wpad], 0.0,
                         writes=[wq_b], sem="vv")
                k.dma("sync", w_f.ap[:, 0:wreal],
                      W["l1_w2"][r0:r0 + 128, c0:c0 + wreal], "sw",
                      writes=[w_f])
                quant_chain(w_f, wq_b.ap[:, 0:wreal], 128, wreal,
                            scq.ap[:, 4:5], wq_b)
                _transpose16(k, pb, ident_b, wq_b, tx_b, n=ntile)
                out = w2T_q[c0:c0 + ntile * 128,
                            ci * 128:(ci + 1) * 128].rearrange(
                    "(dc p) c -> p dc c", p=128)
                in_ = tx_b.ap[:, 0:ntile * 128].rearrange(
                    "p (dc c) -> p dc c", c=128)
                k.dma("gpsimd", out, in_, "sq", reads=[tx_b])
        v_qcache = ("sq", k.val("sq"))

        for g in range(8):
            G = [2 * g, 2 * g + 1]
            scr = [i for i in range(NTC) if i not in G]
            xT_t = [scr[0], scr[1]]
            w1_t = [scr[2], scr[3]]
            w3_t = [scr[4], scr[5]]
            gT_t = scr[6:12]
            w2_t = [scr[12], scr[13], G[0], G[1], scr[2], scr[3]]
            xr_t = [scr[4], scr[5]]

            items = []
            for tc in range(2):
                outv = bfv(xT_t[tc]).rearrange("p (b c) -> p b c", c=128)
                inv = xt_il[:, G[tc] * 128:(G[tc] + 1) * 128].rearrange(
                    "(b p) c -> p b c", p=128)
                items.append((outv, inv))
            k.dma_batch("sync", "sx", items,
                        writes=[x_tm[xT_t[0]], x_tm[xT_t[1]]],
                        extra=[v_spill])

            for ht, (h0, wdt) in enumerate(HT):
                for dram, tiles, eng, sem in ((w1T_q, w1_t, "sync", "sh"),
                                              (w3T_q, w3_t, "scalar", "sh2")):
                    items = []
                    for half in range(2):
                        outv = bfv(tiles[half])[:, 0:8 * 512].rearrange(
                            "p (dc c) -> p dc c", c=512)[:, :, 0:wdt]
                        inv = dram[half * 1024:(half + 1) * 1024,
                                   h0:h0 + wdt].rearrange(
                            "(dc p) c -> p dc c", p=128)
                        items.append((outv, inv))
                    k.dma_batch(eng, sem, items,
                                writes=[x_tm[tiles[0]], x_tm[tiles[1]]],
                                extra=[v_qcache])
                for tc in range(2):
                    for pi, tiles in enumerate((w1_t, w3_t)):
                        bank = pb[pi]
                        grp_reads = [x_tm[xT_t[0]], x_tm[xT_t[1]],
                                     x_tm[tiles[0]], x_tm[tiles[1]]]
                        first = True
                        for dc in range(16):
                            til = tiles[dc // 8]
                            rhs = bfv(til)[:, (dc % 8) * 512:
                                           (dc % 8) * 512 + wdt]
                            for hl in range(2):
                                lhs = bfv(xT_t[tc])[
                                    :, (2 * dc + hl) * 128:
                                    (2 * dc + hl + 1) * 128]
                                last = (dc == 15 and hl == 1)
                                k.do("tensor", "matmul", bank.ap[:, 0:wdt],
                                     lhs, rhs, start=first, stop=last,
                                     reads=(grp_reads if last else
                                            [x_tm[xT_t[tc]], x_tm[til]]),
                                     writes=[bank],
                                     sem="mm" if last else None)
                                first = False
                    k.do("scalar", "activation", out=t_a.ap[:, 0:wdt],
                         in_=pb[0].ap[:, 0:wdt], func=AF.Silu,
                         scale=scq.ap[:, 1:2], reads=[pb[0], scq],
                         writes=[t_a], sem="aa")
                    k.do("vector", "tensor_scalar", t_b.ap[:, 0:wdt],
                         pb[1].ap[:, 0:wdt], scq.ap[:, 3:4], None, ALU.mult,
                         reads=[pb[1], scq], writes=[t_b], sem="vv")
                    k.do("vector", "tensor_mul", t_a.ap[:, 0:wdt],
                         t_a.ap[:, 0:wdt], t_b.ap[:, 0:wdt], reads=[t_b],
                         writes=[t_a], sem="vv")
                    k.do("scalar", "activation",
                         out=t_b.ap[:, 0:wdt // 2].bitcast(BF16)[:, 0:wdt],
                         in_=t_a.ap[:, 0:wdt], func=AF.Identity, reads=[t_a],
                         writes=[t_b], sem="aa")
                    k.do("vector", "tensor_sub",
                         t_b.ap[:, 1024:1024 + wdt // 2].bitcast(BF16)[:, 0:wdt],
                         t_a.ap[:, 0:wdt],
                         t_b.ap[:, 0:wdt // 2].bitcast(BF16)[:, 0:wdt],
                         reads=[t_a], writes=[t_b], sem="vv")
                    nt_all = wdt // 128
                    for hcg in range((nt_all + 3) // 4):
                        for hl in range(2):
                            bank = pb[4]
                            nt = min(4, nt_all - hcg * 4)
                            for qq in range(nt):
                                ii = hcg * 4 + qq
                                srcap = t_b.ap[:, hl * 1024:hl * 1024 +
                                               wdt // 2].bitcast(BF16)[
                                    :, ii * 128:(ii + 1) * 128]
                                k.do("tensor", "matmul",
                                     bank.ap[:, qq * 128:(qq + 1) * 128],
                                     srcap, ident_b.ap[:, :], start=True,
                                     stop=True, reads=[t_b, ident_b],
                                     writes=[bank],
                                     sem="mm" if qq == nt - 1 else None)
                            for qq in range(nt):
                                hc = (h0 // 128) + hcg * 4 + qq
                                dst = gT_t[hc // 8]
                                dcol = (hc % 8) * 512 + tc * 256 + hl * 128
                                k.do("scalar", "activation",
                                     out=bfv(dst)[:, dcol:dcol + 128],
                                     in_=bank.ap[:, qq * 128:(qq + 1) * 128],
                                     func=AF.Identity, reads=[bank],
                                     writes=[x_tm[dst]], sem="aa")

            k.dma_batch("sync", "sx",
                        [(x_tm[xr_t[tc]].ap[:, :],
                          spill_x[G[tc] * 128:(G[tc] + 1) * 128, :])
                         for tc in range(2)],
                        writes=[x_tm[xr_t[0]], x_tm[xr_t[1]]],
                        extra=[v_spill])
            for et in range(4):
                ecols = slice(et * 512, (et + 1) * 512)
                items = {"sync": [], "scalar": []}
                wr = {"sync": [], "scalar": []}
                for ti in range(6):
                    h0 = ti * 1024
                    nb = min(8, NC1 - ti * 8)
                    if nb <= 0:
                        break
                    outv = bfv(w2_t[ti])[:, 0:nb * 512].rearrange(
                        "p (b c) -> p b c", c=512)
                    inv = w2T_q[h0:h0 + nb * 128, ecols].rearrange(
                        "(b p) c -> p b c", p=128)
                    eng = "sync" if ti < 3 else "scalar"
                    items[eng].append((outv, inv))
                    wr[eng].append(x_tm[w2_t[ti]])
                k.dma_batch("sync", "sh", items["sync"], writes=wr["sync"],
                            extra=[v_qcache])
                k.dma_batch("scalar", "sh2", items["scalar"],
                            writes=wr["scalar"], extra=[v_qcache])
                for tc in range(2):
                    bank = pb[2 + tc]
                    grp_reads = ([x_tm[t] for t in gT_t]
                                 + [x_tm[t] for t in w2_t[:6]])
                    first = True
                    for hc in range(NC1):
                        rhs = bfv(w2_t[hc // 8])[:, (hc % 8) * 512:
                                                 (hc % 8) * 512 + 512]
                        gti = gT_t[hc // 8]
                        base = (hc % 8) * 512 + tc * 256
                        for hl in range(2):
                            lhs = bfv(gti)[:, base + hl * 128:
                                           base + (hl + 1) * 128]
                            last = (hc == NC1 - 1 and hl == 1)
                            k.do("tensor", "matmul", bank.ap[:, :], lhs, rhs,
                                 start=first, stop=last,
                                 reads=(grp_reads if last else
                                        [x_tm[gti], x_tm[w2_t[hc // 8]]]),
                                 writes=[bank], sem="mm" if last else None)
                            first = False
                    xr = x_tm[xr_t[tc]]
                    k.do("vector", "scalar_tensor_tensor",
                         xr.ap[:, ecols], bank.ap[:, :], scq.ap[:, 5:6],
                         xr.ap[:, ecols], ALU.mult, ALU.add,
                         reads=[bank, scq], writes=[xr], sem="vv")
            _layernorm(k, [x_tm[xr_t[0]], x_tm[xr_t[1]]], msc, t_a, sq)
            k.dma_batch("gpsimd", "so",
                        [(spill_x[G[tc] * 128:(G[tc] + 1) * 128, :],
                          x_tm[xr_t[tc]].ap[:, :]) for tc in range(2)],
                        reads=[x_tm[xr_t[0]], x_tm[xr_t[1]]])

        k.dma_batch("sync", "sx",
                    [(x_tm[i].ap[:, :], spill_x[i * 128:(i + 1) * 128, :])
                     for i in range(NTC)], writes=x_tm,
                    extra=[("so", k.val("so"))])


def _make_consts(core_id):
    c = {}
    c["ident_b"] = np.eye(128, dtype=BF)
    c["ident_f"] = np.eye(128, dtype=np.float32)
    c["ones_f"] = np.ones((128, 512), dtype=np.float32)
    c["flag"] = np.full((128, 1), float(core_id % 2), np.float32)
    c["flagi"] = np.full((128, 1), 1.0 - float(core_id % 2), np.float32)
    t0 = (core_id % 2) * T
    pos = np.arange(t0, t0 + T, dtype=np.float64)
    j = np.arange(64, dtype=np.float64)
    theta = 10000.0 ** (-2.0 * j / 128.0)
    ang = pos[None, :] * theta[:, None]
    cos = np.cos(ang).astype(np.float32)
    sin = np.sin(ang).astype(np.float32)
    c["cs_t"] = np.concatenate([cos, sin], axis=0)
    c["sc_t"] = np.concatenate([sin, cos], axis=0)
    ey = np.eye(64, dtype=np.float32)
    m1 = np.concatenate([ey, -ey], axis=0)
    m2 = np.concatenate([ey, ey], axis=0)
    c["mrot"] = np.concatenate([m1, m2], axis=1)
    loglam = -(2.0 ** (-8.0 * (np.arange(NH, dtype=np.float64) + 1.0) / NH))
    i = np.arange(CH, dtype=np.float64)
    dm = np.zeros((128, NH * 128), np.float32)
    for h in range(NH):
        diff = i[None, :] - i[:, None]
        m = np.where(diff >= 0, np.exp(loglam[h] * diff), 0.0)
        dm[:, h * 128:(h + 1) * 128] = m.astype(np.float32)
    c["dmask"] = dm
    c["deck"] = np.exp(loglam[None, :] * (CH - 1.0 - i[:, None])).astype(np.float32)
    c["decq"] = np.exp(loglam[None, :] * (i[:, None] + 1.0)).astype(np.float32)
    return c


def kernel(**inputs):
    x = np.ascontiguousarray(np.asarray(inputs["L0_attn_x"], np.float32)
                             .reshape(4 * 4096, D))
    wmap = {
        "l0_wq": "L0_attn_Wq", "l0_wk": "L0_attn_Wk", "l0_wv": "L0_attn_Wv",
        "l0_wo": "L0_attn_Wo", "l0_wg": "L0_ffn_w_gate", "l0_wu": "L0_ffn_w_up",
        "l0_wd": "L0_ffn_w_down", "l1_wq": "L1_attn_Wq", "l1_wk": "L1_attn_Wk",
        "l1_wv": "L1_attn_Wv", "l1_wo": "L1_attn_Wo", "l1_wg1": "L1_attn_Wg1",
        "l1_wg2": "L1_attn_Wg2", "l1_w1": "L1_ffn_w1", "l1_w3": "L1_ffn_w3",
        "l1_w2": "L1_ffn_w2",
    }
    ws = {kk: np.ascontiguousarray(np.asarray(inputs[v], np.float32))
          for kk, v in wmap.items()}
    bq = np.asarray(inputs["L0_attn_bq"], np.float32)
    bk = np.asarray(inputs["L0_attn_bk"], np.float32)
    bv = np.asarray(inputs["L0_attn_bv"], np.float32)
    bo = np.asarray(inputs["L0_attn_bo"], np.float32)
    nw = np.asarray(inputs["L1_attn_norm_weight"], np.float32)

    nc = _build_program()

    in_maps = []
    for c in range(NCORES):
        xs = x[c * T:(c + 1) * T]
        m = dict(ws)
        m["x_in"] = xs
        m["xT_in"] = np.ascontiguousarray(xs.T)
        m.update(_make_consts(c))
        m["bq_sb"] = np.ascontiguousarray(bq.reshape(NH, 128).T)
        m["bk_sb"] = np.ascontiguousarray(bk.reshape(NH, 128).T)
        m["bv_sb"] = np.ascontiguousarray(bv.reshape(NH, 128).T)
        m["bo_rep"] = np.ascontiguousarray(np.broadcast_to(bo, (128, D)))
        m["nw_rep"] = np.ascontiguousarray(np.broadcast_to(nw, (128, D)))
        in_maps.append(m)

    res = run_bass_kernel_spmd(nc, in_maps, core_ids=list(range(NCORES)))
    out = np.concatenate([r["y_out"] for r in res.results], axis=0)
    return out.reshape(4, 4096, D)
